# version 14
# baseline (speedup 1.0000x reference)
import contextlib
import numpy as np
import ml_dtypes
import concourse.bass as bass
import concourse.mybir as mybir
from concourse.bass_utils import run_bass_kernel_spmd

F32 = mybir.dt.float32
BF16 = mybir.dt.bfloat16
AF = mybir.ActivationFunctionType
ALU = mybir.AluOpType
AX = mybir.AxisListType

NSEQ = 4
S = 2048
D = 1024
DC = 8
NH = 16
HPAIRS = 8
HD = 64
NEXP = 8
HID = 4096
NT = 4
T5 = 512

DEBUG_TAPS: list = []
STOP_AFTER: str = ""


class Sem:
    __slots__ = ("name", "h", "count")

    def __init__(self, name):
        self.name = name
        self.h = None
        self.count = 0


class Ctx:
    def __init__(self, nc):
        self.nc = nc
        self.sems = []
        self.cur = None
        self.e = None
        self.waited = {}
        self.taps = {}
        self.g = {}
        self.stopped = False

    def sem(self, name):
        s = Sem(name)
        self.sems.append(s)
        return s

    def begin_pass(self, name, eng):
        self.cur = name
        self.e = eng
        self.waited = {}
        self.g = {}
        self.stopped = False
        for s in self.sems:
            s.count = 0

    def wait(self, eng, sem, val):
        if val <= 0:
            return
        k = (eng, sem.name)
        if self.waited.get(k, 0) >= val:
            return
        self.waited[k] = val
        if self.cur == eng:
            self.e.wait_ge(sem.h, val)

    def wait_ev(self, eng, ev):
        if ev is not None:
            self.wait(eng, ev[0], ev[1])

    def op(self, eng, emit, inc=None, n=1):
        if self.cur == eng:
            ins = emit(self.e)
            if inc is not None:
                ins.then_inc(inc.h, n)
        if inc is not None:
            inc.count += n
            return (inc, inc.count)
        return None

    def gw(self, eng, key):
        self.wait_ev(eng, self.g.get(key))

    def gs(self, key, ev):
        self.g[key] = ev

    def join(self, engines, sems):
        for eng in engines:
            for s in sems:
                self.wait(eng, s, s.count)


ENGS = ("pe", "act", "dve", "gp", "sp")

VC_ANW, VC_MNW, VC_BO = 0, 8, 16
VC_BQ, VC_BK, VC_BV = 24, 25, 26
VC_EPSLN, VC_EPSRMS = 27, 28

FA_SQ0 = 0
FA_SQ1 = 512
FA_BCA = 1024
FA_BCB = 1536
FA_BCC = 2048
FA_RT0 = 2048
FA_RT1 = 2056
FA_RT2 = 2064
FA_MX = 2560
FA_MB = 2624
FA_M1 = 2626
FA_M2 = 2627
FA_RW1 = 2628
FA_RW2 = 2629
FA_ROW0 = 3072
FA_ROW1 = 3584

BB_YB = 0
BB_QRAW = 0
BB_KRAW = 512
BB_EXP0 = 1024
BB_EXP1 = 1536
BB_WRT = 2048

CC_WBUF = 0
CC_QTR = 2048
CC_KTR = 4096
CC_AOP = 6144
CC_WVT = 8192


def build_nc():
    nc = bass.Bass("TRN2", target_bir_lowering=False)
    ctx = Ctx(nc)

    d = {}
    def din(name, shape, dtype):
        d[name] = nc.dram_tensor(name, shape, dtype, kind="ExternalInput").ap()
    din("xT", [NSEQ, D, S], F32)
    din("a_wq", [D, HD], F32)
    din("a_wk", [D, HD], F32)
    din("a_wv", [D, HD], F32)
    din("a_wo", [HD, D], F32)
    din("f_wqT", [D, D], F32)
    din("f_wkT", [D, D], F32)
    din("f_wvT", [D, D], F32)
    din("f_woT", [D, D], F32)
    din("gateT", [D, NEXP], F32)
    din("w1T", [NEXP, D, HID], F32)
    din("w2T", [NEXP, HID, D], F32)
    din("b1f", [128, NEXP * 32], F32)
    din("b2", [NEXP, D], F32)
    din("vecs", [128, 64], F32)
    din("cst32", [128, 256], F32)
    din("cst16", [128, 1280], BF16)
    din("maskbig", [128, 1280], BF16)
    din("cossin", [128, 4096], BF16)

    out = nc.dram_tensor("out", [NSEQ, D, S], F32, kind="ExternalOutput").ap()

    wS4 = nc.dram_tensor("wS4", [4 * D, D], BF16, kind="Internal").ap()
    wS1 = nc.dram_tensor("wS1", [NEXP, D, HID], BF16, kind="Internal").ap()
    wS2 = nc.dram_tensor("wS2", [NEXP, HID, D], BF16, kind="Internal").ap()
    yS = nc.dram_tensor("yS", [D, S], BF16, kind="Internal").ap()
    tnS = nc.dram_tensor("tnS", [D, S], BF16, kind="Internal").ap()
    aoS = nc.dram_tensor("aoS", [D, S], BF16, kind="Internal").ap()

    with contextlib.ExitStack() as stack:
        ec = stack.enter_context
        R = ec(nc.sbuf_tensor("R", [128, DC * S], F32))
        AF32 = ec(nc.sbuf_tensor("AF32", [128, 4096], F32))
        AB16 = ec(nc.sbuf_tensor("AB16", [128, 4096], BF16))
        A16a = ec(nc.sbuf_tensor("A16a", [128, 16384], BF16))
        A16b = ec(nc.sbuf_tensor("A16b", [128, 16384], BF16))
        A16c = ec(nc.sbuf_tensor("A16c", [128, 16384], BF16))
        cst32 = ec(nc.sbuf_tensor("cst32b", [128, 256], F32))
        cst16 = ec(nc.sbuf_tensor("cst16b", [128, 1280], BF16))
        maskb = ec(nc.sbuf_tensor("maskbH", [128, 1280], BF16))
        cossin = ec(nc.sbuf_tensor("cossinb", [128, 4096], BF16))
        aqkv = ec(nc.sbuf_tensor("aqkv", [128, 3 * 512], BF16))
        awo = ec(nc.sbuf_tensor("awo", [64, 1024], BF16))
        vecs = ec(nc.sbuf_tensor("vecsb", [128, 64], F32))
        b1b = ec(nc.sbuf_tensor("b1b", [128, 256], F32))
        b2b = ec(nc.sbuf_tensor("b2b", [8, 1024], BF16))
        gateb = ec(nc.sbuf_tensor("gateb", [128, 64], BF16))

        pp = [ec(nc.psum_tensor(f"pp{i}", [128, 512], F32)) for i in range(4)]
        avb = ec(nc.psum_tensor("avbp", [128, 512], F32))
        rowA = ec(nc.psum_tensor("rowAp", [128, 512], F32))
        rowB = ec(nc.psum_tensor("rowBp", [128, 512], F32))
        bcb = ec(nc.psum_tensor("bcbp", [128, 512], F32))

        sIn = ctx.sem("sIn")
        sPrep = ctx.sem("sPrep")
        sSt = ctx.sem("sSt")
        sW = [ctx.sem("sW0"), ctx.sem("sW1")]
        sPE = [ctx.sem("sPE0"), ctx.sem("sPE1")]
        sACT = [ctx.sem("sACT0"), ctx.sem("sACT1")]
        sDVE = [ctx.sem("sDVE0"), ctx.sem("sDVE1")]
        for s in ctx.sems:
            s.h = ec(nc.semaphore(s.name))

        T = dict(R=R, AF32=AF32, AB16=AB16, A16a=A16a, A16b=A16b, A16c=A16c,
                 cst32=cst32, cst16=cst16, maskb=maskb, cossin=cossin,
                 aqkv=aqkv, awo=awo, vecs=vecs, b1b=b1b, b2b=b2b, gateb=gateb,
                 pp=pp, avb=avb, rowA=rowA, rowB=rowB, bcb=bcb,
                 d=d, out=out, wS4=wS4, wS1=wS1, wS2=wS2, yS=yS, tnS=tnS,
                 aoS=aoS, sIn=sIn, sPrep=sPrep, sSt=sSt, sW=sW, sPE=sPE,
                 sACT=sACT, sDVE=sDVE, nc=nc)

        block = ec(nc.Block())

        def mk(engname):
            def body(eng):
                ctx.begin_pass(engname, eng)
                program(ctx, T)
            return body
        block.sync(mk("sp"))
        block.tensor(mk("pe"))
        block.scalar(mk("act"))
        block.vector(mk("dve"))
        block.gpsimd(mk("gp"))

    return nc, ctx



def tap(ctx, T, name, ap_fn, shape, np_dtype, ev_list):
    if name not in DEBUG_TAPS:
        return
    key = "dbg_" + name
    if key not in ctx.taps:
        dt = BF16 if np_dtype == ml_dtypes.bfloat16 else F32
        ctx.taps[key] = T["nc"].dram_tensor(key, shape, dt, kind="ExternalOutput").ap()
    dst = ctx.taps[key]
    for ev in ev_list:
        ctx.wait_ev("gp", ev)
    phase_join(ctx, T, engines=("gp",))
    ctx.op("gp", lambda e: e.dma_start(out=dst, in_=ap_fn()), inc=T["sSt"], n=16)


def stop_here(ctx, label):
    if STOP_AFTER and label == STOP_AFTER:
        ctx.stopped = True
    return ctx.stopped


def phase_join(ctx, T, engines=ENGS):
    comp = [T["sPE"][0], T["sPE"][1], T["sACT"][0], T["sACT"][1],
            T["sDVE"][0], T["sDVE"][1]]
    full = comp + [T["sW"][0], T["sW"][1], T["sIn"], T["sSt"]]
    for eng in engines:
        ctx.join([eng], comp if eng in ("pe", "act", "dve") else full)


def mm_chain(ctx, T, par, bank_key, psum_ap, pairs, tp=None):
    ctx.gw("pe", bank_key)
    n = len(pairs)
    ev = None
    for i, (lt, rh) in enumerate(pairs):
        kw = {"tile_position": tp} if tp is not None else {}
        inc_s = T["sPE"][par] if i == n - 1 else None
        ev = ctx.op("pe", lambda e, lt=lt, rh=rh, i=i, kw=kw, n=n: e.matmul(
            psum_ap, lt, rh, start=(i == 0), stop=(i == n - 1),
            skip_group_check=True, **kw), inc=inc_s)
    return ev



def program(ctx, T):
    prep(ctx, T)
    if stop_here(ctx, "prep"):
        finale(ctx, T)
        return
    for seq in range(NSEQ):
        ev_r = load_x(ctx, T, seq)
        done = False
        for call in (0, 1):
            lbl = f"s{seq}c{call}"
            ev_r, ev_tn1 = phaseA(ctx, T, seq, call, ev_r)
            if stop_here(ctx, "A" + lbl):
                done = True
                break
            ev_r = phaseB(ctx, T, seq, call, ev_r, ev_tn1)
            if stop_here(ctx, "B" + lbl):
                done = True
                break
            ev_r = phaseC(ctx, T, seq, call, ev_r)
            if stop_here(ctx, "C" + lbl):
                done = True
                break
            if call == 0:
                ev_r = phase_mid(ctx, T, seq, ev_r)
                if stop_here(ctx, "M" + lbl):
                    done = True
                    break
        store_out(ctx, T, seq, ev_r)
        if done:
            break
    finale(ctx, T)


def prep(ctx, T):
    d = T["d"]
    sIn, sPrep = T["sIn"], T["sPrep"]
    for dst, src in [
        (T["cst32"][:, :], d["cst32"]),
        (T["cst16"][:, :], d["cst16"]),
        (T["maskb"][:, :], d["maskbig"]),
        (T["cossin"][:, :], d["cossin"]),
        (T["vecs"][:, :], d["vecs"]),
        (T["b1b"][:, :], d["b1f"]),
    ]:
        ctx.op("sp", lambda e, dst=dst, src=src: e.dma_start(out=dst, in_=src),
               inc=sIn, n=16)
    gtmp = T["AF32"][:, FA_SQ0:FA_SQ0 + 64]
    ctx.op("sp", lambda e: e.dma_start(
        out=gtmp.rearrange("p (c e) -> p c e", c=DC),
        in_=d["gateT"].rearrange("(c p) e -> p c e", p=128)), inc=sIn, n=16)
    ctx.wait("gp", sIn, sIn.count)
    ev = ctx.op("gp", lambda e: e.tensor_copy(out=T["gateb"][:, :], in_=gtmp),
                inc=sPrep)
    ctx.gs("sq0", ev)
    for i, nm in enumerate(["a_wq", "a_wk", "a_wv"]):
        ctx.op("gp", lambda e, i=i, nm=nm: e.dma_start(
            out=T["aqkv"][:, i * 512:(i + 1) * 512].rearrange("p (c o) -> p c o", c=DC),
            in_=d[nm].rearrange("(c p) o -> p c o", p=128)), inc=sPrep, n=16)
    ctx.op("gp", lambda e: e.dma_start(out=T["awo"][:, :], in_=d["a_wo"]), inc=sPrep, n=16)
    ctx.op("gp", lambda e: e.dma_start(out=T["b2b"][:, :], in_=d["b2"]), inc=sPrep, n=16)
    ctx.g["prep_attn"] = (sPrep, sPrep.count)
    ev = None
    for i, nm in enumerate(["f_wqT", "f_wkT", "f_wvT", "f_woT"]):
        ev = ctx.op("gp", lambda e, i=i, nm=nm: e.dma_start(
            out=T["wS4"][i * D:(i + 1) * D, :], in_=d[nm]), inc=sPrep, n=16)
    ctx.g["prep_ws4"] = ev
    for ee in range(NEXP):
        ctx.op("gp", lambda e, ee=ee: e.dma_start(
            out=T["wS1"][ee], in_=d["w1T"][ee]), inc=sPrep, n=16)
        ev = ctx.op("gp", lambda e, ee=ee: e.dma_start(
            out=T["wS2"][ee], in_=d["w2T"][ee]), inc=sPrep, n=16)
    ctx.g["prep_all"] = ev


def load_x(ctx, T, seq):
    ctx.wait("sp", T["sSt"], T["sSt"].count)
    return ctx.op("sp", lambda e: e.dma_start(
        out=T["R"].rearrange("p (c t) -> p c t", c=DC),
        in_=T["d"]["xT"][seq].rearrange("(c p) t -> p c t", p=128)), inc=T["sIn"], n=16)


def store_out(ctx, T, seq, ev_r):
    ctx.wait_ev("gp", ev_r)
    phase_join(ctx, T, engines=("gp",))
    ctx.op("gp", lambda e: e.dma_start(
        out=T["out"][seq].rearrange("(c p) t -> p c t", p=128),
        in_=T["R"].rearrange("p (c t) -> p c t", c=DC)), inc=T["sSt"], n=16)


def finale(ctx, T):
    ctx.wait("gp", T["sSt"], T["sSt"].count)
    ctx.wait("sp", T["sSt"], T["sSt"].count)



def _rows_rstd(ctx, T, par, eps_col, with_mean=False, s1b=None, s2b=None,
               r0c=FA_ROW0, r1c=FA_ROW1, msqc=FA_SQ0, rkey="rows",
               slkey="rows_sl"):
    AF32, vecs = T["AF32"], T["vecs"]
    sDVE, sACT = T["sDVE"][par], T["sACT"][par]
    s1b = s1b if s1b is not None else T["rowA"]
    s2b = s2b if s2b is not None else T["rowB"]
    r0 = AF32[0:1, r0c:r0c + 512]
    r1 = AF32[0:1, r1c:r1c + 512]
    msq = AF32[0:1, msqc:msqc + 512]
    invD = 1.0 / D
    ctx.gw("dve", slkey)
    if with_mean:
        ctx.op("dve", lambda e: e.tensor_scalar_mul(out=r0, in0=s1b[0:1, :],
                                                    scalar1=invD))
        ctx.op("dve", lambda e: e.tensor_scalar_mul(out=r1, in0=s2b[0:1, :],
                                                    scalar1=invD))
        ev = ctx.op("dve", lambda e: e.tensor_mul(out=msq, in0=r0, in1=r0),
                    inc=sDVE)
        ctx.gs(rkey, ev)
        ev = ctx.op("dve", lambda e: e.tensor_sub(out=r1, in0=r1, in1=msq), inc=sDVE)
        ctx.wait_ev("act", ev)
        ctx.op("act", lambda e: e.activation(
            out=r1, in_=r1, func=AF.Ln, bias=vecs[0:1, eps_col:eps_col + 1]))
        ev = ctx.op("act", lambda e: e.activation(
            out=r1, in_=r1, func=AF.Exp, scale=-0.5), inc=sACT)
        ctx.wait_ev("dve", ev)
        ev = ctx.op("dve", lambda e: e.scalar_tensor_tensor(
            out=r0, in0=r0, scalar=-1.0, in1=r1, op0=ALU.mult, op1=ALU.mult), inc=sDVE)
    else:
        ev = ctx.op("dve", lambda e: e.tensor_scalar_mul(
            out=r1, in0=s2b[0:1, :], scalar1=invD), inc=sDVE)
        ctx.gs(rkey, ev)
        ctx.wait_ev("act", ev)
        ctx.op("act", lambda e: e.activation(
            out=r1, in_=r1, func=AF.Ln, bias=vecs[0:1, eps_col:eps_col + 1]))
        ev = ctx.op("act", lambda e: e.activation(
            out=r1, in_=r1, func=AF.Exp, scale=-0.5), inc=sACT)
    return ev


def _bcast_row(ctx, T, par, row_ap, evac_to=None, evac_key=None, bank=None,
               bkey="bcb"):
    ones_row = T["cst32"][0:1, 0:128]
    bank = bank if bank is not None else T["bcb"]
    ctx.gw("pe", bkey)
    ev = ctx.op("pe", lambda e: e.matmul(bank[:, :], ones_row, row_ap,
                                         start=True, stop=True, skip_group_check=True),
                inc=T["sPE"][par])
    if evac_to is None:
        return ev, bank[:, :]
    ctx.wait_ev("act", ev)
    if evac_key is not None:
        ctx.gw("act", evac_key)
    ev = ctx.op("act", lambda e: e.copy(out=evac_to, in_=bank[:, :]),
                inc=T["sACT"][par])
    ctx.gs(bkey, ev)
    return ev, evac_to


def _ln_stats(ctx, T, par, src_tiles, with_mean, eps_col, s1b=None, s2b=None,
              s1key="rows", s2key="rowsB", r0c=FA_ROW0, r1c=FA_ROW1,
              msqc=FA_SQ0, slkey="rows_sl", sqa=None, sqkeys=("sqA0", "sqA1")):
    AF32, cst32 = T["AF32"], T["cst32"]
    sPE, sACT = T["sPE"][par], T["sACT"][par]
    ones_col = cst32[:, 0:1]
    ones_bf = T["cst16"][:, 128:129]
    s1b = s1b if s1b is not None else T["rowA"]
    s2b = s2b if s2b is not None else T["rowB"]
    if sqa is None:
        sqa = T["A16c"][:, 0:1024]
    ev_s2 = None
    pend = []

    def emit_s2(dc, ev_sq, sq, key):
        nonlocal ev_s2
        ctx.wait_ev("pe", ev_sq)
        if dc == 0:
            ctx.gw("pe", s2key)
        ev_s2 = ctx.op("pe", lambda e, dc=dc, sq=sq: e.matmul(
            s2b[0:1, :], ones_bf, sq, start=(dc == 0), stop=(dc == DC - 1),
            skip_group_check=True), inc=sPE)
        ctx.gs(key, ev_s2)

    for dc in range(DC):
        sq = sqa[:, (dc % 2) * 512:(dc % 2) * 512 + 512]
        key = sqkeys[dc % 2]
        src = src_tiles(dc)
        ctx.gw("act", key)
        ev_sq = ctx.op("act", lambda e, src=src, sq=sq: e.activation(
            out=sq, in_=src, func=AF.Square), inc=sACT)
        if dc == 0:
            ctx.gw("pe", s1key)
        if with_mean:
            ctx.op("pe", lambda e, src=src, dc=dc: e.matmul(
                s1b[0:1, :], ones_col, src, start=(dc == 0), stop=(dc == DC - 1),
                skip_group_check=True))
        pend.append((dc, ev_sq, sq, key))
        if len(pend) == 2:
            emit_s2(*pend.pop(0))
    emit_s2(*pend.pop(0))
    ctx.wait_ev("dve", ev_s2)
    ev = _rows_rstd(ctx, T, par, eps_col, with_mean=with_mean, s1b=s1b,
                    s2b=s2b, r0c=r0c, r1c=r1c, msqc=msqc, rkey=s1key,
                    slkey=slkey)
    ctx.gs(s2key, ev)
    return ev


def phaseA(ctx, T, seq, call, ev_r):
    par = call
    phase_join(ctx, T)
    for eng in ("pe", "act", "dve"):
        ctx.wait_ev(eng, ev_r)
    R, AF32, AB16, A16a = T["R"], T["AF32"], T["AB16"], T["A16a"]
    vecs = T["vecs"]
    sDVE, sSt = T["sDVE"][par], T["sSt"]
    S1B = [T["rowA"], T["pp"][2]]
    S2B = [T["rowB"], T["pp"][3]]
    S1K = ["rows", "pp2"]
    S2K = ["rowsB", "pp3"]
    R0C = [FA_ROW0, FA_BCC]
    R1C = [FA_ROW1, 2560]
    BCB = [T["bcb"], T["avb"]]
    BCK = ["bcb", "avb"]
    BCAC = [FA_BCA, FA_BCB]
    BAK = ["bca", "bcb2"]
    SLK = ["rows_sl", "rows_sl2"]
    ev_fold = None
    ev_tn1 = None
    A16b, A16c = T["A16b"], T["A16c"]
    sW = T["sW"][par]
    ctx.wait_ev("sp", ctx.g.get("prep_ws4"))
    ctx.op("sp", lambda e: e.dma_start(
        out=A16c[:, CC_WVT:CC_WVT + 8192].rearrange("p (c o) -> p c o", c=DC),
        in_=T["wS4"][2 * D:3 * D].rearrange("(c p) o -> p c o", p=128)),
        inc=sW, n=16)
    ev_wv = (sW, sW.count)

    tn1_evs = [None] * NT

    def vbuild(t):
        ctx.wait_ev("pe", ev_wv)
        ctx.wait_ev("pe", tn1_evs[t])
        for tc in range(4 * t, 4 * t + 4):
            for half in range(2):
                bk = half
                pairs = [(A16a[:, dc * S + tc * 128: dc * S + tc * 128 + 128],
                          A16c[:, CC_WVT + dc * 1024 + half * 512:
                               CC_WVT + dc * 1024 + half * 512 + 512])
                         for dc in range(DC)]
                ev = mm_chain(ctx, T, par, f"pp{bk}", T["pp"][bk][:, :], pairs)
                ctx.wait_ev("act", ev)
                ev_v = ctx.op("act", lambda e, bk=bk, tc=tc, half=half: e.copy(
                    out=A16b[:, tc * 1024 + half * 512:
                             tc * 1024 + half * 512 + 512],
                    in_=T["pp"][bk][:, :]), inc=T["sACT"][par])
                ctx.gs(f"pp{bk}", ev_v)

    def rtile(t, dc):
        return R[:, dc * S + t * T5: dc * S + t * T5 + T5]

    def ytile(dc):
        return AB16[:, BB_YB + dc * 512: BB_YB + dc * 512 + 512]

    def ln_rows(t):
        p = t % 2
        ev = _ln_stats(ctx, T, par, lambda dc, t=t: rtile(t, dc), True, VC_EPSLN,
                       s1b=S1B[p], s2b=S2B[p], s1key=S1K[p], s2key=S2K[p],
                       r0c=R0C[p], r1c=R1C[p], msqc=FA_SQ0, slkey=SLK[p])
        if t >= 2:
            vbuild(t - 2)
        ctx.wait_ev("pe", ev)
        evA, bcA = _bcast_row(ctx, T, par, AF32[0:1, R1C[p]:R1C[p] + 512],
                              evac_to=AF32[:, BCAC[p]:BCAC[p] + 512],
                              evac_key=BAK[p], bank=BCB[p], bkey=BCK[p])
        evB, bcB = _bcast_row(ctx, T, par, AF32[0:1, R0C[p]:R0C[p] + 512],
                              bank=BCB[p], bkey=BCK[p])
        ctx.gs(SLK[p], evB)
        return evA, bcA, evB, bcB

    def y_and_tn(t, evA, bcA, evB, bcB):
        nonlocal ev_fold, ev_tn1
        p = t % 2
        ctx.wait_ev("dve", evA)
        ctx.wait_ev("dve", evB)
        ytmp = AF32[:, FA_SQ1:FA_SQ1 + 512]
        ctx.gw("dve", "sq1")
        ev_y = None
        for dc in range(DC):
            rsl = rtile(t, dc)
            ybd = ytile(dc)
            ctx.op("dve", lambda e, rsl=rsl, ytmp=ytmp, bcA=bcA: e.tensor_mul(
                out=ytmp, in0=rsl, in1=bcA))
            ev_y = ctx.op("dve", lambda e, ybd=ybd, ytmp=ytmp, bcB=bcB: e.tensor_add(
                out=ybd, in0=ytmp, in1=bcB), inc=sDVE)
            ev_fold = ctx.op("dve", lambda e, rsl=rsl, ybd=ybd: e.scalar_tensor_tensor(
                out=rsl, in0=ybd, scalar=0.5, in1=rsl, op0=ALU.mult, op1=ALU.add),
                inc=sDVE)
        ctx.gs(BCK[p], ev_y)
        ctx.gs(BAK[p], ev_y)
        ctx.gs("sq1", ev_fold)
        ctx.wait_ev("act", ev_fold)
        ev = _ln_stats(ctx, T, par, lambda dc: ytile(dc), False, VC_EPSRMS,
                       s1b=S1B[p], s2b=S2B[p], s1key=S1K[p], s2key=S2K[p],
                       r0c=R0C[p], r1c=R1C[p], msqc=FA_SQ0, slkey=SLK[p])
        ctx.wait_ev("pe", ev)
        evA2, bcA2 = _bcast_row(ctx, T, par, AF32[0:1, R1C[p]:R1C[p] + 512],
                                evac_to=AF32[:, BCAC[p]:BCAC[p] + 512],
                                evac_key=BAK[p], bank=BCB[p], bkey=BCK[p])
        ctx.gs(SLK[p], evA2)
        ctx.wait_ev("dve", evA2)
        ttmp = AF32[:, FA_SQ1:FA_SQ1 + 512]
        ctx.gw("dve", "sq1")
        for dc in range(DC):
            ybd = ytile(dc)
            tnd = A16a[:, dc * S + t * T5: dc * S + t * T5 + T5]
            ctx.op("dve", lambda e, ybd=ybd, ttmp=ttmp, bcA2=bcA2: e.tensor_mul(
                out=ttmp, in0=ybd, in1=bcA2))
            ev_tn1 = ctx.op("dve", lambda e, tnd=tnd, dc=dc, ttmp=ttmp:
                            e.tensor_scalar_mul(
                                out=tnd, in0=ttmp,
                                scalar1=vecs[:, VC_ANW + dc:VC_ANW + dc + 1]),
                            inc=sDVE)
        ctx.gs("sq1", ev_tn1)
        ctx.gs(BAK[p], ev_tn1)
        tn1_evs[t] = ev_tn1
        ctx.wait_ev("sp", ev_tn1)
        ctx.op("sp", lambda e, t=t: e.dma_start(
            out=T["yS"].rearrange("(c p) t -> p c t", p=128)[:, :, t * T5:t * T5 + T5],
            in_=AB16[:, BB_YB:BB_YB + 4096].rearrange("p (c t) -> p c t", c=DC)),
            inc=sSt, n=16)
        ctx.wait("dve", sSt, sSt.count)

    pend = None
    for t in range(NT):
        got = ln_rows(t)
        if pend is not None:
            y_and_tn(pend[0], *pend[1])
        pend = (t, got)
    y_and_tn(pend[0], *pend[1])
    vbuild(NT - 2)
    vbuild(NT - 1)
    tap(ctx, T, f"A_R_{seq}_{call}", lambda: T["R"][:, :], [128, DC * S],
        np.float32, [ev_fold])
    tap(ctx, T, f"A_TN1_{seq}_{call}", lambda: A16a[:, :], [128, 16384],
        ml_dtypes.bfloat16, [ev_tn1])
    return ev_fold, ev_tn1


def _softmax_pair(ctx, T, par, heads, kT_fn, qT_fn, v_fn, ao_fn,
                  qfull=None, kfull=None, sqarea=None, bias_col=None):
    AF32, AB16, maskb, cst16 = T["AF32"], T["AB16"], T["maskb"], T["cst16"]
    sPE, sACT, sDVE = T["sPE"][par], T["sACT"][par], T["sDVE"][par]
    ones_bf = cst16[:, 128:129]
    rot = [1, 2, 3]
    mxbase = {0: FA_MX, 64: FA_BCC}
    m1col = {0: FA_M1, 64: FA_M2}
    mbcol = {0: FA_MB, 64: FA_MB + 1}
    mbkey = {0: "mb0", 64: "mb1"}
    ebufs = {0: [(BB_EXP0, "exp0"), (BB_EXP1, "exp1")],
             64: [(BB_QRAW, "qraw"), (BB_KRAW, "kraw")]}
    dnbank = {0: T["rowA"], 64: T["rowB"]}
    dnkey = {0: "rows", 64: "rowsB"}
    avbank = {0: T["pp"][0], 64: T["avb"]}
    avkey = {0: "pp0", 64: "avb"}
    rowsl = {0: FA_ROW0, 64: FA_ROW1}
    bcsl = {0: (FA_BCA, "bca"), 64: (FA_BCB, "bcb2")}

    hi = heads[-1] + 64
    sqq = sqarea[0:hi, 0:2048]
    sqk = sqarea[0:hi, 2048:4096]
    ctx.gw("act", "sqs0")
    ev_sq = ctx.op("act", lambda e: e.activation(out=sqq, in_=qfull,
                                                 func=AF.Square), inc=sACT)
    ctx.gw("act", "sqs1")
    ev_sk = ctx.op("act", lambda e: e.activation(out=sqk, in_=kfull,
                                                 func=AF.Square), inc=sACT)
    nbanks = [T["rowA"], T["rowB"], T["pp"][0], T["avb"]]
    nkeys = ["rows", "rowsB", "pp0", "avb"]
    ones_bf = cst16[:, 128:129]
    ev = None
    for h in heads:
        for mi, (sq, evs) in enumerate(((sqq, ev_sq), (sqk, ev_sk))):
            ctx.wait_ev("pe", evs)
            for sl in range(4):
                ctx.gw("pe", nkeys[sl])
                ev = ctx.op("pe", lambda e, h=h, sq=sq, sl=sl: e.matmul(
                    nbanks[sl][0:1, :], cst16[h:h + 64, 128:129],
                    sq[h:h + 64, sl * 512:sl * 512 + 512], start=True, stop=True,
                    skip_group_check=True, tile_position=(h, 0)), inc=sPE)
            ctx.wait_ev("dve", ev)
            nc_ = mxbase[h] + mi * 8
            for sl in range(4):
                ev = ctx.op("dve", lambda e, sl=sl, nc_=nc_: e.reduce_max(
                    out=AF32[0:1, nc_ + sl:nc_ + sl + 1],
                    in_=nbanks[sl][0:1, :], axis=AX.X), inc=sDVE)
                ctx.gs(nkeys[sl], ev)
            ctx.wait_ev("dve", ev)
            ev = ctx.op("dve", lambda e, nc_=nc_: e.reduce_max(
                out=AF32[0:1, nc_ + 4:nc_ + 5], in_=AF32[0:1, nc_:nc_ + 4],
                axis=AX.X), inc=sDVE)
            ctx.wait_ev("dve", ev)
    ctx.gs("sqs0", ev)
    ctx.gs("sqs1", ev)
    for h in heads:
        mb = mxbase[h]
        mc = m1col[h]
        ev = ctx.op("dve", lambda e, mb=mb, mc=mc: e.tensor_mul(
            out=AF32[0:1, mc:mc + 1], in0=AF32[0:1, mb + 4:mb + 5],
            in1=AF32[0:1, mb + 12:mb + 13]), inc=sDVE)
        ctx.wait_ev("act", ev)
        ev = ctx.op("act", lambda e, mc=mc: e.activation(
            out=AF32[0:1, mc:mc + 1], in_=AF32[0:1, mc:mc + 1], func=AF.Sqrt),
            inc=sACT)
        ctx.wait_ev("dve", ev)
        ev = ctx.op("dve", lambda e, mc=mc: e.tensor_scalar_mul(
            out=AF32[0:1, mc:mc + 1], in0=AF32[0:1, mc:mc + 1], scalar1=-0.125),
            inc=sDVE)
        ctx.wait_ev("pe", ev)
        ctx.gw("pe", "bcb")
        ev = ctx.op("pe", lambda e, mc=mc: e.matmul(
            T["bcb"][:, 0:1], T["cst32"][0:1, 0:128], AF32[0:1, mc:mc + 1],
            start=True, stop=True, skip_group_check=True), inc=sPE)
        ctx.wait_ev("dve", ev)
        ctx.gw("dve", mbkey[h])
        ev_mb = ctx.op("dve", lambda e, h=h: e.tensor_copy(
            out=AF32[:, mbcol[h]:mbcol[h] + 1], in_=T["bcb"][:, 0:1]), inc=sDVE)
        ctx.gs("bcb", ev_mb)
        ctx.g["_mbev" + str(h)] = ev_mb

    ev_ao = None
    last_exp = None
    for qs in range(NT):
        nkc = 4 * qs + 4
        ev_e = {h: [None] * nkc for h in heads}

        def emit_sT(h, kc, qs=qs):
            bk = rot[(kc * len(heads) + heads.index(h)) % 3]
            return bk, mm_chain(ctx, T, par, f"pp{bk}", T["pp"][bk][:, :],
                                [(kT_fn(h, kc), qT_fn(h, qs))], tp=(h, 0))

        def emit_exp(h, kc, bk, ev_s, qs=qs):
            boff, ekey = ebufs[h][kc % 2]
            eb = AB16[:, boff:boff + 512]
            ctx.wait_ev("act", ev_s)
            ctx.wait_ev("act", ctx.g["_mbev" + str(h)])
            ctx.gw("act", ekey)
            eve = ctx.op("act", lambda e, bk=bk, eb=eb, h=h: e.activation(
                out=eb, in_=T["pp"][bk][:, :], func=AF.Exp, scale=0.125,
                bias=AF32[:, mbcol[h]:mbcol[h] + 1]), inc=sACT)
            ctx.gs(f"pp{bk}", eve)
            if kc >= 4 * qs:
                off = 384 - 128 * (kc - 4 * qs)
                ctx.wait_ev("dve", eve)
                eve = ctx.op("dve", lambda e, eb=eb, off=off: e.tensor_mul(
                    out=eb, in0=eb, in1=maskb[:, off:off + 512]), inc=sDVE)
            return eve

        def emit_cons(h, kc, qs=qs, nkc=nkc):
            boff, ekey = ebufs[h][kc % 2]
            eb = AB16[:, boff:boff + 512]
            if kc >= 4 * qs:
                ctx.wait("pe", sDVE, ev_e[h][kc][1])
            else:
                ctx.wait("pe", sACT, ev_e[h][kc][1])
            if kc == 0:
                ctx.gw("pe", dnkey[h])
                ctx.gw("pe", avkey[h])
            ctx.op("pe", lambda e, eb=eb, kc=kc, h=h: e.matmul(
                dnbank[h][0:1, :], ones_bf, eb, start=(kc == 0),
                stop=(kc == nkc - 1), skip_group_check=True))
            ev = ctx.op("pe", lambda e, eb=eb, kc=kc, h=h: e.matmul(
                avbank[h][h:h + 64, :], v_fn(h, kc), eb,
                start=(kc == 0), stop=(kc == nkc - 1), skip_group_check=True,
                tile_position=(0, h)), inc=sPE)
            ctx.gs(ekey, ev)
            return ev

        ev_av = {}
        for kc in range(nkc):
            sts = [(h,) + emit_sT(h, kc) for h in heads]
            for h, bk, ev_s in sts:
                ev_e[h][kc] = emit_exp(h, kc, bk, ev_s)
            if kc >= 1:
                for h in heads:
                    ev_av[h] = emit_cons(h, kc - 1)
        for h in heads:
            ev_av[h] = emit_cons(h, nkc - 1)
            last_exp = ev_e[h][nkc - 1]
        for h in heads:
            rs = rowsl[h]
            ctx.wait_ev("act", ev_av[h])
            ctx.gw("act", "rows_sl" if h == 0 else "rows_sl2")
            ctx.op("act", lambda e, h=h, rs=rs: e.activation(
                out=AF32[0:1, rs:rs + 512], in_=dnbank[h][0:1, :], func=AF.Ln))
            ev = ctx.op("act", lambda e, rs=rs: e.activation(
                out=AF32[0:1, rs:rs + 512], in_=AF32[0:1, rs:rs + 512],
                func=AF.Exp, scale=-1.0), inc=sACT)
            ctx.gs(dnkey[h], ev)
            ctx.g["_dn" + str(h)] = ev
        for h in heads:
            rs = rowsl[h]
            dst, dkey = bcsl[h]
            ctx.wait_ev("pe", ctx.g["_dn" + str(h)])
            evb, bcR = _bcast_row(ctx, T, par, AF32[0:1, rs:rs + 512],
                                  evac_to=AF32[:, dst:dst + 512], evac_key=dkey)
            ctx.gs("rows_sl" if h == 0 else "rows_sl2", evb)
            ctx.g["_bc" + str(h)] = (evb, bcR)
        for h in heads:
            evb, bcR = ctx.g["_bc" + str(h)]
            dst, dkey = bcsl[h]
            ctx.wait_ev("dve", evb)
            aop = ao_fn(h, qs)
            if bias_col is None:
                ev_ao = ctx.op("dve", lambda e, aop=aop, h=h, bcR=bcR: e.tensor_mul(
                    out=aop, in0=avbank[h][h:h + 64, :], in1=bcR[h:h + 64, :]),
                    inc=sDVE)
            else:
                ctx.op("dve", lambda e, aop=aop, h=h, bcR=bcR: e.tensor_mul(
                    out=aop, in0=avbank[h][h:h + 64, :], in1=bcR[h:h + 64, :]))
                ev_ao = ctx.op("dve", lambda e, aop=aop: e.tensor_scalar_add(
                    out=aop, in0=aop,
                    scalar1=T["vecs"][0:64, bias_col:bias_col + 1]), inc=sDVE)
            ctx.gs(avkey[h], ev_ao)
            ctx.gs(dkey, ev_ao)
    for h in heads:
        ctx.gs(mbkey[h], last_exp)
    return ev_ao


def phaseB(ctx, T, seq, call, ev_r, ev_tn1):
    par = call
    phase_join(ctx, T)
    R, AF32, AB16 = T["R"], T["AF32"], T["AB16"]
    A16a, A16b, A16c = T["A16a"], T["A16b"], T["A16c"]
    cst16, cossin, vecs = T["cst16"], T["cossin"], T["vecs"]
    sPE, sACT, sDVE = T["sPE"][par], T["sACT"][par], T["sDVE"][par]
    sW, sSt = T["sW"][par], T["sSt"]
    WQ, WK, WV, WO = 0, D, 2 * D, 3 * D


    rot = cst16[:, 0:128]
    qTr = A16c[:, CC_QTR:CC_QTR + 2048]
    kTr = A16c[:, CC_KTR:CC_KTR + 2048]
    aopair = A16c[:, CC_AOP:CC_AOP + 2048]
    wbuf = A16c[:, CC_WBUF:CC_WBUF + 2048]
    ev_proj = None
    for hp in range(HPAIRS):
        ctx.gw("sp", "wbuf")
        for i, woff in enumerate([WQ, WK]):
            ctx.op("sp", lambda e, woff=woff, i=i, hp=hp: e.dma_start(
                out=wbuf[:, i * 1024:(i + 1) * 1024].rearrange("p (c o) -> p c o", c=DC),
                in_=T["wS4"][woff:woff + D].rearrange("(c p) o -> p c o", p=128)
                [:, :, hp * 128:(hp + 1) * 128]), inc=sW, n=16)
        ctx.wait("pe", sW, sW.count)
        for t in range(NT):
            ts = t * T5
            for which in range(2):
                bk = which
                dst = qTr if which == 0 else kTr
                rkey = "qraw" if which == 0 else "kraw"
                raw = AB16[:, (BB_QRAW if which == 0 else BB_KRAW):][:, 0:512]
                ctx.gw("dve", "qkTr")
                pairs = [(wbuf[:, which * 1024 + dc * 128: which * 1024 + dc * 128 + 128],
                          A16a[:, dc * S + ts: dc * S + ts + T5]) for dc in range(DC)]
                ev = mm_chain(ctx, T, par, f"pp{bk}", T["pp"][bk][:, :], pairs)
                ctx.wait_ev("act", ev)
                ctx.gw("act", rkey)
                ev_raw = ctx.op("act", lambda e, bk=bk, raw=raw: e.copy(
                    out=raw, in_=T["pp"][bk][:, :]), inc=sACT)
                ctx.gs(f"pp{bk}", ev_raw)
                ctx.wait_ev("pe", ev_raw)
                ctx.gw("pe", "bcb")
                ev_rot = ctx.op("pe", lambda e, raw=raw: e.matmul(
                    T["bcb"][:, :], rot, raw, start=True, stop=True,
                    skip_group_check=True), inc=sPE)
                ctx.wait_ev("dve", ev_rot)
                cosb = cossin[:, ts:ts + T5]
                sinb = cossin[:, 2048 + ts:2048 + ts + T5]
                ctx.gw("dve", "sq1")
                ctx.op("dve", lambda e, raw=raw, cosb=cosb: e.tensor_mul(
                    out=AF32[:, FA_SQ1:FA_SQ1 + 512], in0=raw, in1=cosb))
                ctx.gw("dve", "bcc")
                ctx.op("dve", lambda e, sinb=sinb: e.tensor_mul(
                    out=AF32[:, FA_BCC:FA_BCC + 512], in0=T["bcb"][:, :], in1=sinb))
                ev_proj = ctx.op("dve", lambda e, dst=dst, ts=ts: e.tensor_add(
                    out=dst[:, ts:ts + T5], in0=AF32[:, FA_SQ1:FA_SQ1 + 512],
                    in1=AF32[:, FA_BCC:FA_BCC + 512]), inc=sDVE)
                ctx.gs("bcb", ev_proj)
                ctx.gs(rkey, ev_proj)
                ctx.gs("sq1", ev_proj)
                ctx.gs("bcc", ev_proj)
        ctx.gs("wbuf", (sPE, sPE.count))
        ctx.wait_ev("pe", ev_proj)
        ctx.gw("dve", "aopair")
        ev_ao = _softmax_pair(
            ctx, T, par, [0, 64],
            kT_fn=lambda h, kc: kTr[h:h + 64, kc * 128:kc * 128 + 128],
            qT_fn=lambda h, qs: qTr[h:h + 64, qs * T5:qs * T5 + T5],
            v_fn=lambda h, kc, hp=hp: A16b[:, kc * 1024 + (2 * hp) * 64 + h:
                                           kc * 1024 + (2 * hp) * 64 + h + 64],
            ao_fn=lambda h, qs: aopair[h:h + 64, qs * T5:qs * T5 + T5],
            qfull=qTr, kfull=kTr, sqarea=A16c[:, 8192:12288])
        ctx.gs("qkTr", (sPE, sPE.count))
        ctx.wait_ev("gp", ev_ao)
        ev_sp = ctx.op("gp", lambda e, hp=hp: e.dma_start(
            out=T["aoS"][hp * 128:(hp + 1) * 128, :], in_=aopair), inc=sSt, n=16)
        ctx.gs("aopair", ev_sp)

    tap(ctx, T, f"B_ao_{seq}_{call}", lambda: T["aoS"][:, :], [D, S],
        ml_dtypes.bfloat16, [])
    ctx.wait("sp", sPE, sPE.count)
    ctx.op("sp", lambda e: e.dma_start(
        out=A16b[:, 0:8192].rearrange("p (c o) -> p c o", c=DC),
        in_=T["wS4"][WO:WO + D].rearrange("(c p) o -> p c o", p=128)), inc=sW, n=16)
    aot = A16c[:, 0:4096]
    yt = A16c[:, 4096:8192]
    htile = A16a[:, 0:4096]
    tn2t = A16a[:, 4096:8192]
    ones_col = T["cst32"][:, 0:1]
    ev_rupd = None
    ev_wr = None
    def emit_routing(t):
        nonlocal ev_wr
        ts = t * T5
        ctx.wait_ev("pe", ev_tn2)
        for tsub in range(4):
            off = tsub * 128
            pairs = [(tn2t[:, dc * 512 + off: dc * 512 + off + 128],
                      T["gateb"][:, dc * 8: dc * 8 + 8]) for dc in range(DC)]
            ev_lg = mm_chain(ctx, T, par, "avb", T["avb"][:, 0:8], pairs)
            lg = T["avb"][:, 0:8]
            ctx.wait_ev("dve", ev_lg)
            ctx.gw("dve", "rt0")
            def dsync(emit):
                ev = ctx.op("dve", emit, inc=sDVE)
                ctx.wait_ev("dve", ev)
                return ev
            dsync(lambda e, lg=lg: e.reduce_max(
                out=AF32[:, FA_M1:FA_M1 + 1], in_=lg, axis=AX.X))
            dsync(lambda e, lg=lg: e.tensor_scalar(
                out=AF32[:, FA_RT0:FA_RT0 + 8], in0=lg,
                scalar1=AF32[:, FA_M1:FA_M1 + 1], scalar2=1.0, op0=ALU.is_ge,
                op1=ALU.mult))
            ev = dsync(lambda e, lg=lg: e.scalar_tensor_tensor(
                out=AF32[:, FA_RT1:FA_RT1 + 8], in0=AF32[:, FA_RT0:FA_RT0 + 8],
                scalar=-1e30, in1=lg, op0=ALU.mult, op1=ALU.add))
            ctx.gs("avb", ev)
            dsync(lambda e: e.reduce_max(
                out=AF32[:, FA_M2:FA_M2 + 1], in_=AF32[:, FA_RT1:FA_RT1 + 8],
                axis=AX.X))
            ev = dsync(lambda e: e.tensor_sub(
                out=AF32[:, FA_RW1:FA_RW1 + 1], in0=AF32[:, FA_M1:FA_M1 + 1],
                in1=AF32[:, FA_M2:FA_M2 + 1]))
            ctx.wait_ev("act", ev)
            ev = ctx.op("act", lambda e: e.activation(
                out=AF32[:, FA_RW1:FA_RW1 + 1], in_=AF32[:, FA_RW1:FA_RW1 + 1],
                func=AF.Sigmoid), inc=sACT)
            ctx.wait_ev("dve", ev)
            dsync(lambda e: e.tensor_scalar(
                out=AF32[:, FA_RW2:FA_RW2 + 1], in0=AF32[:, FA_RW1:FA_RW1 + 1],
                scalar1=-1.0, scalar2=1.0, op0=ALU.mult, op1=ALU.add))
            dsync(lambda e: e.tensor_scalar(
                out=AF32[:, FA_RT2:FA_RT2 + 8], in0=AF32[:, FA_RT1:FA_RT1 + 8],
                scalar1=AF32[:, FA_M2:FA_M2 + 1], scalar2=1.0, op0=ALU.is_ge,
                op1=ALU.mult))
            dsync(lambda e: e.tensor_scalar_mul(
                out=AF32[:, FA_RT0:FA_RT0 + 8], in0=AF32[:, FA_RT0:FA_RT0 + 8],
                scalar1=AF32[:, FA_RW1:FA_RW1 + 1]))
            dsync(lambda e: e.tensor_scalar_mul(
                out=AF32[:, FA_RT2:FA_RT2 + 8], in0=AF32[:, FA_RT2:FA_RT2 + 8],
                scalar1=AF32[:, FA_RW2:FA_RW2 + 1]))
            ev = dsync(lambda e: e.tensor_add(
                out=AF32[:, FA_RT0:FA_RT0 + 8], in0=AF32[:, FA_RT0:FA_RT0 + 8],
                in1=AF32[:, FA_RT2:FA_RT2 + 8]))
            if f"B_wrp_{seq}_{call}" in DEBUG_TAPS:
                ctx.op("dve", lambda e, t=t, tsub=tsub: e.tensor_copy(
                    out=A16c[:, 8192 + (t * 4 + tsub) * 8: 8192 + (t * 4 + tsub) * 8 + 8],
                    in_=AF32[:, FA_RT0:FA_RT0 + 8]), inc=sDVE)
            if f"B_lg_{seq}_{call}" in DEBUG_TAPS:
                ctx.op("dve", lambda e, t=t, tsub=tsub: e.tensor_copy(
                    out=A16c[:, 8320 + (t * 4 + tsub) * 8: 8320 + (t * 4 + tsub) * 8 + 8],
                    in_=lg), inc=sDVE)
            ctx.wait_ev("pe", ev)
            ctx.gw("pe", "bcb")
            ev = ctx.op("pe", lambda e: e.transpose(
                T["bcb"][0:8, 0:128], AF32[:, FA_RT0:FA_RT0 + 8],
                T["cst32"][:, 128:256]), inc=sPE)
            ctx.gs("rt0", ev)
            ctx.wait_ev("dve", ev)
            ev_wr = ctx.op("dve", lambda e, ts=ts, off=off: e.tensor_copy(
                out=AB16[0:8, BB_WRT + ts + off: BB_WRT + ts + off + 128],
                in_=T["bcb"][0:8, 0:128]), inc=sDVE)
            ctx.gs("bcb", ev_wr)
        ctx.gs("tn2t", (sPE, sPE.count))
    for t in range(NT):
        ts = t * T5
        ctx.wait("sp", sSt, sSt.count)
        ctx.gw("sp", "aot")
        ctx.op("sp", lambda e, ts=ts: e.dma_start(
            out=aot.rearrange("p (c t) -> p c t", c=DC),
            in_=T["aoS"].rearrange("(c p) t -> p c t", p=128)[:, :, ts:ts + T5]),
            inc=sW, n=16)
        ctx.gw("sp", "yt")
        ctx.op("sp", lambda e, ts=ts: e.dma_start(
            out=yt.rearrange("p (c t) -> p c t", c=DC),
            in_=T["yS"].rearrange("(c p) t -> p c t", p=128)[:, :, ts:ts + T5]),
            inc=sW, n=16)
        ctx.wait("pe", sW, sW.count)
        ev_s2 = None
        pend_s2 = []

        def emit_s2(oc, ev_sq, sq, skey):
            nonlocal ev_s2
            ctx.wait_ev("pe", ev_sq)
            if oc == 0:
                ctx.gw("pe", "rowsB")
            ev_s2 = ctx.op("pe", lambda e, oc=oc, sq=sq: e.matmul(
                T["rowB"][0:1, :], T["cst16"][:, 128:129], sq, start=(oc == 0),
                stop=(oc == DC - 1), skip_group_check=True), inc=sPE)
            ctx.gs(skey, ev_s2)

        for oc in range(DC):
            bk = oc % 2
            pairs = [(A16b[:, dc * 1024 + oc * 128: dc * 1024 + oc * 128 + 128],
                      aot[:, dc * 512: dc * 512 + 512]) for dc in range(DC)]
            ev = mm_chain(ctx, T, par, f"pp{bk}", T["pp"][bk][:, :], pairs)
            rsl = R[:, oc * S + ts: oc * S + ts + T5]
            hsl = htile[:, oc * 512: oc * 512 + 512]
            ysl = yt[:, oc * 512: oc * 512 + 512]
            ctx.wait_ev("dve", ev)
            ev_rupd = ctx.op("dve", lambda e, bk=bk, rsl=rsl: e.scalar_tensor_tensor(
                out=rsl, in0=T["pp"][bk][:, :], scalar=0.5, in1=rsl,
                op0=ALU.mult, op1=ALU.add), inc=sDVE)
            ev_h = ctx.op("dve", lambda e, bk=bk, hsl=hsl, ysl=ysl: e.tensor_add(
                out=hsl, in0=T["pp"][bk][:, :], in1=ysl), inc=sDVE)
            ctx.gs(f"pp{bk}", ev_h)
            skey = "sqB0" if oc % 2 == 0 else "sqB1"
            sq = A16c[:, 8192 + (oc % 2) * 512: 8192 + (oc % 2) * 512 + 512]
            ctx.wait_ev("act", ev_h)
            ctx.gw("act", skey)
            ev_sq = ctx.op("act", lambda e, hsl=hsl, sq=sq: e.activation(
                out=sq, in_=hsl, func=AF.Square), inc=sACT)
            pend_s2.append((oc, ev_sq, sq, skey))
            if len(pend_s2) == 2:
                emit_s2(*pend_s2.pop(0))
        emit_s2(*pend_s2.pop(0))
        ctx.gs("aot", (sPE, sPE.count))
        ctx.gs("yt", ev_h)
        if t >= 1:
            emit_routing(t - 1)
        ctx.wait_ev("dve", ev_s2)
        ev_rows = _rows_rstd(ctx, T, par, VC_EPSRMS, with_mean=False,
                             rkey="rowsB")
        ctx.wait_ev("pe", ev_rows)
        evH, bcH = _bcast_row(ctx, T, par, AF32[0:1, FA_ROW1:FA_ROW1 + 512],
                              evac_to=AF32[:, FA_BCB:FA_BCB + 512], evac_key="bcb2")
        ctx.gs("rows_sl", evH)
        ctx.wait_ev("dve", evH)
        ev_tn2 = None
        ctx.gw("dve", "sq1")
        ctx.gw("dve", "tn2t")
        for dc in range(DC):
            hsl = htile[:, dc * 512: dc * 512 + 512]
            tsl = tn2t[:, dc * 512: dc * 512 + 512]
            ctx.op("dve", lambda e, hsl=hsl: e.tensor_mul(
                out=AF32[:, FA_SQ1:FA_SQ1 + 512], in0=hsl, in1=bcH))
            ev_tn2 = ctx.op("dve", lambda e, tsl=tsl, dc=dc: e.tensor_scalar_mul(
                out=tsl, in0=AF32[:, FA_SQ1:FA_SQ1 + 512],
                scalar1=vecs[:, VC_MNW + dc:VC_MNW + dc + 1]), inc=sDVE)
        ctx.gs("sq1", ev_tn2)
        ctx.gs("bcb2", ev_tn2)
        ctx.wait_ev("gp", ev_tn2)
        ctx.op("gp", lambda e, ts=ts: e.dma_start(
            out=T["tnS"].rearrange("(c p) t -> p c t", p=128)[:, :, ts:ts + T5],
            in_=tn2t.rearrange("p (c t) -> p c t", c=DC)), inc=sSt, n=16)
    emit_routing(NT - 1)
    tap(ctx, T, f"B_R_{seq}_{call}", lambda: T["R"][:, :], [128, DC * S],
        np.float32, [ev_rupd])
    tap(ctx, T, f"B_tn2_{seq}_{call}", lambda: T["tnS"][:, :], [D, S],
        ml_dtypes.bfloat16, [])
    tap(ctx, T, f"B_wrt_{seq}_{call}",
        lambda: AB16[0:8, BB_WRT:BB_WRT + 2048], [8, 2048],
        ml_dtypes.bfloat16, [ev_wr])
    tap(ctx, T, f"B_wrp_{seq}_{call}", lambda: A16c[:, 8192:8192 + 128],
        [128, 128], ml_dtypes.bfloat16, [ev_wr])
    tap(ctx, T, f"B_lg_{seq}_{call}", lambda: A16c[:, 8320:8320 + 128],
        [128, 128], ml_dtypes.bfloat16, [ev_wr])
    return ev_rupd


def phaseC(ctx, T, seq, call, ev_r):
    par = call
    phase_join(ctx, T)
    R, AF32, AB16 = T["R"], T["AF32"], T["AB16"]
    A16a, A16b, A16c = T["A16a"], T["A16b"], T["A16c"]
    sPE, sACT, sDVE = T["sPE"][par], T["sACT"][par], T["sDVE"][par]
    sW, sSt = T["sW"][par], T["sSt"]
    b1b, b2b, cst16 = T["b1b"], T["b2b"], T["cst16"]
    tn2buf = A16a[:, 0:8192]
    w1buf = [A16a[:, 8192:12288], A16a[:, 12288:16384]]
    ev_rupd = None
    for t10 in range(2):
        t0 = t10 * 1024
        ctx.wait("sp", sSt, sSt.count)
        ctx.wait_ev("sp", ctx.g.get("prep_all"))
        ctx.gw("sp", "tn2buf")
        ctx.op("sp", lambda e, t0=t0: e.dma_start(
            out=tn2buf.rearrange("p (c t) -> p c t", c=DC),
            in_=T["tnS"].rearrange("(c p) t -> p c t", p=128)[:, :, t0:t0 + 1024]),
            inc=sW, n=16)
        ev_tn_ld = (sW, sW.count)
        for hcsl in range(8):
            h0 = hcsl * 512
            for eq in range(2):
                ev_w1s = [None, None]

                def w1_load(e4, eq=eq, h0=h0):
                    ee = eq * 4 + e4
                    ctx.gw("sp", f"w1b{e4 % 2}")
                    ctx.op("sp", lambda e, ee=ee, h0=h0, e4=e4: e.dma_start(
                        out=w1buf[e4 % 2].rearrange("p (c h) -> p c h", c=DC),
                        in_=T["wS1"][ee].rearrange("(c p) h -> p c h", p=128)
                        [:, :, h0:h0 + 512]), inc=sW, n=16)
                    ev_w1s[e4 % 2] = (sW, sW.count)
                w1_load(0)
                w1_load(1)
                ctx.gw("sp", "w2q")
                for e4b in range(4):
                    eeb = eq * 4 + e4b
                    ctx.op("sp", lambda e, eeb=eeb, h0=h0, e4b=e4b: e.dma_start(
                        out=A16b[:, e4b * 4096:(e4b + 1) * 4096]
                        .rearrange("p (c d) -> p c d", c=4),
                        in_=T["wS2"][eeb][h0:h0 + 512, :]
                        .rearrange("(c p) d -> p c d", p=128)), inc=sW, n=16)
                ev_w2 = (sW, sW.count)
                ev_hid = None
                for e4 in range(4):
                    ee = eq * 4 + e4
                    if e4 >= 2:
                        w1_load(e4)
                    ev_wb = None
                    selbanks = [T["bcb"], T["rowA"], T["rowB"]]
                    selkeys = ["bcb", "rows", "rowsB"]
                    for half in range(2):
                        sb = (e4 * 2 + half) % 3
                        ctx.gw("pe", selkeys[sb])
                        ev = ctx.op("pe", lambda e, ee=ee, t0=t0, half=half, sb=sb,
                                    selbanks=selbanks: e.matmul(
                            selbanks[sb][:, :],
                            cst16[0:8, 256 + ee * 128: 256 + ee * 128 + 128],
                            AB16[0:8, BB_WRT + t0 + half * 512:
                                 BB_WRT + t0 + half * 512 + 512],
                            start=True, stop=True, skip_group_check=True), inc=sPE)
                        ctx.wait_ev("act", ev)
                        wkey = "bca" if half == 0 else "bcb2"
                        dstw = AF32[:, (FA_BCA if half == 0 else FA_BCB):][:, 0:512]
                        ctx.gw("act", wkey)
                        ev_wb = ctx.op("act", lambda e, dstw=dstw, sb=sb,
                                       selbanks=selbanks: e.copy(
                            out=dstw, in_=selbanks[sb][:, :]), inc=sACT)
                        ctx.gs(selkeys[sb], ev_wb)
                    ctx.wait("pe", ev_w1s[e4 % 2][0], ev_w1s[e4 % 2][1])
                    ctx.wait("pe", ev_tn_ld[0], ev_tn_ld[1])
                    upbanks = [T["pp"][0], T["pp"][1], T["avb"]]
                    upkeys = ["pp0", "pp1", "avb"]
                    for hs in range(4):
                        for ts2 in range(2):
                            ub = (hs * 2 + ts2) % 3
                            pairs = [(w1buf[e4 % 2][:, dc * 512 + hs * 128:
                                                    dc * 512 + hs * 128 + 128],
                                      tn2buf[:, dc * 1024 + ts2 * 512:
                                             dc * 1024 + ts2 * 512 + 512])
                                     for dc in range(DC)]
                            ev = mm_chain(ctx, T, par, upkeys[ub], upbanks[ub][:, :],
                                          pairs)
                            gsl = (hs * 2 + ts2) % 3
                            gkey = ["sq0", "sq1", "sq2"][gsl]
                            gl = AF32[:, [FA_SQ0, FA_SQ1, FA_BCC][gsl]:][:, 0:512]
                            bc = b1b[:, ee * 32 + hcsl * 4 + hs:
                                     ee * 32 + hcsl * 4 + hs + 1]
                            ctx.wait_ev("act", ev)
                            ctx.gw("act", gkey)
                            ev_g = ctx.op("act", lambda e, ub=ub, gl=gl, bc=bc,
                                          upbanks=upbanks: e.activation(
                                              out=gl, in_=upbanks[ub][:, :],
                                              func=AF.Gelu, bias=bc), inc=sACT)
                            ctx.gs(upkeys[ub], ev_g)
                            ctx.wait_ev("dve", ev_g)
                            ctx.wait_ev("dve", ev_wb)
                            wbs = AF32[:, (FA_BCA if ts2 == 0 else FA_BCB):][:, 0:512]
                            hidsl = A16c[:, e4 * 4096 + hs * 1024 + ts2 * 512:
                                         e4 * 4096 + hs * 1024 + ts2 * 512 + 512]
                            ev_hid = ctx.op("dve", lambda e, gl=gl, wbs=wbs, hidsl=hidsl:
                                            e.tensor_mul(out=hidsl, in0=gl, in1=wbs),
                                            inc=sDVE)
                            ctx.gs(gkey, ev_hid)
                    ctx.gs("bca", ev_hid)
                    ctx.gs("bcb2", ev_hid)
                    ctx.gs(f"w1b{e4 % 2}", (sPE, sPE.count))
                ctx.wait("pe", ev_w2[0], ev_w2[1])
                ctx.wait("pe", sDVE, ev_hid[1])
                for dc in range(DC):
                    for ts2 in range(2):
                        bk = 2 + ((dc * 2 + ts2) % 2)
                        pairs = []
                        for e4 in range(4):
                            for hs in range(4):
                                pairs.append((
                                    A16b[:, e4 * 4096 + hs * 1024 + dc * 128:
                                         A16b.shape[-1] * 0 + e4 * 4096 + hs * 1024 + dc * 128 + 128],
                                    A16c[:, e4 * 4096 + hs * 1024 + ts2 * 512:
                                         e4 * 4096 + hs * 1024 + ts2 * 512 + 512]))
                        if hcsl == 7 and eq == 1:
                            pairs.append((
                                b2b[0:8, dc * 128:(dc + 1) * 128],
                                AB16[0:8, BB_WRT + t0 + ts2 * 512:
                                     BB_WRT + t0 + ts2 * 512 + 512]))
                        ev = mm_chain(ctx, T, par, f"pp{bk}", T["pp"][bk][:, :], pairs)
                        rsl = R[:, dc * S + t0 + ts2 * 512:
                                dc * S + t0 + ts2 * 512 + 512]
                        ctx.wait_ev("dve", ev)
                        ev_rupd = ctx.op("dve", lambda e, bk=bk, rsl=rsl:
                                         e.scalar_tensor_tensor(
                                             out=rsl, in0=T["pp"][bk][:, :], scalar=0.5,
                                             in1=rsl, op0=ALU.mult, op1=ALU.add),
                                         inc=sDVE)
                        ctx.gs(f"pp{bk}", ev_rupd)
                ctx.gs("w2q", (sPE, sPE.count))
                for e4 in range(4):
                    pass
        ctx.gs("tn2buf", (sPE, sPE.count))
    tap(ctx, T, f"C_R_{seq}_{call}", lambda: T["R"][:, :], [128, DC * S],
        np.float32, [ev_rupd])
    return ev_rupd


def phase_mid(ctx, T, seq, ev_r):
    par = 0
    phase_join(ctx, T)
    R, AF32 = T["R"], T["AF32"]
    A16a, A16b, A16c = T["A16a"], T["A16b"], T["A16c"]
    aqkv, awo, vecs = T["aqkv"], T["awo"], T["vecs"]
    sPE, sACT, sDVE = T["sPE"][par], T["sACT"][par], T["sDVE"][par]
    for eng in ("pe", "act", "dve"):
        ctx.wait_ev(eng, ctx.g.get("prep_attn"))
    y2 = A16a[:, 0:4096]
    q2 = A16b[:, 0:2048]
    k2 = A16b[:, 2048:4096]
    z2 = A16b[:, 4096:6144]
    v2 = A16c[:, 0:1024]
    for t in range(NT):
        ts = t * T5

        def rtile(dc, ts=ts):
            return R[:, dc * S + ts: dc * S + ts + T5]
        ev_rows = _ln_stats(ctx, T, par, rtile, True, VC_EPSLN,
                            sqa=A16c[:, 1024:2048], sqkeys=("sqM0", "sqM1"))
        ctx.wait_ev("pe", ev_rows)
        evA, bcA = _bcast_row(ctx, T, par, AF32[0:1, FA_ROW1:FA_ROW1 + 512],
                              evac_to=AF32[:, FA_BCA:FA_BCA + 512], evac_key="bca")
        evB, bcB = _bcast_row(ctx, T, par, AF32[0:1, FA_ROW0:FA_ROW0 + 512])
        ctx.gs("rows_sl", evB)
        ctx.wait_ev("dve", evA)
        ctx.wait_ev("dve", evB)
        ctx.gw("dve", "y2")
        ctx.gw("dve", "sq1")
        ev_y2 = None
        for dc in range(DC):
            rsl = rtile(dc)
            ysl = y2[:, dc * 512: dc * 512 + 512]
            ctx.op("dve", lambda e, rsl=rsl: e.tensor_mul(
                out=AF32[:, FA_SQ1:FA_SQ1 + 512], in0=rsl, in1=bcA))
            ev_y2 = ctx.op("dve", lambda e, ysl=ysl: e.tensor_add(
                out=ysl, in0=AF32[:, FA_SQ1:FA_SQ1 + 512], in1=bcB), inc=sDVE)
        ctx.gs("bcb", ev_y2)
        ctx.gs("bca", ev_y2)
        ctx.gs("sq1", ev_y2)
        ctx.wait_ev("pe", ev_y2)
        for which, bias_c in ((0, VC_BQ), (1, VC_BK)):
            bk = which
            pairs = [(aqkv[:, which * 512 + dc * 64: which * 512 + dc * 64 + 64],
                      y2[:, dc * 512: dc * 512 + 512]) for dc in range(DC)]
            ev = mm_chain(ctx, T, par, f"pp{bk}", T["pp"][bk][0:64, :], pairs)
            dst = (q2 if which == 0 else k2)[0:64, ts:ts + T5]
            ctx.wait_ev("act", ev)
            ev_qk = ctx.op("act", lambda e, bk=bk, dst=dst, bias_c=bias_c: e.activation(
                out=dst, in_=T["pp"][bk][0:64, :], func=AF.Identity,
                bias=vecs[0:64, bias_c:bias_c + 1]), inc=sACT)
            ctx.gs(f"pp{bk}", ev_qk)
        for tsub in range(4):
            tc = t * 4 + tsub
            pairs = [(y2[:, dc * 512 + tsub * 128: dc * 512 + tsub * 128 + 128],
                      aqkv[:, 2 * 512 + dc * 64: 2 * 512 + dc * 64 + 64])
                     for dc in range(DC)]
            ev = mm_chain(ctx, T, par, "pp2", T["pp"][2][:, 0:64], pairs)
            ctx.wait_ev("act", ev)
            ev_v2 = ctx.op("act", lambda e, tc=tc: e.copy(
                out=v2[:, tc * 64: tc * 64 + 64], in_=T["pp"][2][:, 0:64]), inc=sACT)
            ctx.gs("pp2", ev_v2)
        ctx.gs("y2", (sPE, sPE.count))
    ctx.wait("pe", sACT, sACT.count)
    ev_ao = _softmax_pair(
        ctx, T, par, [0],
        kT_fn=lambda h, kc: k2[0:64, kc * 128:kc * 128 + 128],
        qT_fn=lambda h, qs: q2[0:64, qs * T5:qs * T5 + T5],
        v_fn=lambda h, kc: v2[:, kc * 64: kc * 64 + 64],
        ao_fn=lambda h, qs: z2[0:64, qs * T5:qs * T5 + T5],
        qfull=q2[0:64, :], kfull=k2[0:64, :], sqarea=A16c[:, 2048:6144],
        bias_col=VC_BV)
    ctx.wait_ev("pe", ev_ao)
    ev_rupd = None
    for t in range(NT):
        ts = t * T5
        for oc in range(DC):
            bk = oc % 2
            ev = mm_chain(ctx, T, par, f"pp{bk}", T["pp"][bk][:, :],
                          [(awo[:, oc * 128:(oc + 1) * 128], z2[0:64, ts:ts + T5])])
            rsl = R[:, oc * S + ts: oc * S + ts + T5]
            ctx.wait_ev("dve", ev)
            ev_rupd = ctx.op("dve", lambda e, bk=bk, rsl=rsl, oc=oc:
                             e.scalar_tensor_tensor(
                                 out=rsl, in0=T["pp"][bk][:, :],
                                 scalar=vecs[:, VC_BO + oc:VC_BO + oc + 1],
                                 in1=rsl, op0=ALU.add, op1=ALU.add), inc=sDVE)
            ctx.gs(f"pp{bk}", ev_rupd)
    tap(ctx, T, f"M_R_{seq}", lambda: T["R"][:, :], [128, DC * S], np.float32,
        [ev_rupd])
    return ev_rupd



def _host_inputs(inp):
    f32 = np.float32
    bf = ml_dtypes.bfloat16
    x = np.asarray(inp["L0_attn_x"])
    xT = np.ascontiguousarray(x.transpose(0, 2, 1))

    def tr(a):
        return np.ascontiguousarray(np.asarray(a, f32).T)
    com = {
        "a_wq": np.ascontiguousarray(np.asarray(inp["L0_attn_wq"], f32)),
        "a_wk": np.ascontiguousarray(np.asarray(inp["L0_attn_wk"], f32)),
        "a_wv": np.ascontiguousarray(np.asarray(inp["L0_attn_wv"], f32)),
        "a_wo": np.ascontiguousarray(np.asarray(inp["L0_attn_wo"], f32)),
        "f_wqT": tr(inp["L0_ffn_wq"]),
        "f_wkT": tr(inp["L0_ffn_wk"]),
        "f_wvT": tr(inp["L0_ffn_wv"]),
        "f_woT": tr(inp["L0_ffn_wo"]),
        "gateT": tr(inp["L0_ffn_gate_w"]),
        "w1T": np.ascontiguousarray(np.asarray(inp["L0_ffn_w1"], f32).transpose(0, 2, 1)),
        "w2T": np.ascontiguousarray(np.asarray(inp["L0_ffn_w2"], f32).transpose(0, 2, 1)),
        "b2": np.ascontiguousarray(np.asarray(inp["L0_ffn_b2"], f32)),
    }
    b1 = np.asarray(inp["L0_ffn_b1"], f32)
    com["b1f"] = np.ascontiguousarray(
        b1.reshape(NEXP, 32, 128).transpose(2, 0, 1).reshape(128, NEXP * 32))
    vecs = np.zeros((128, 64), f32)
    vecs[:, VC_ANW:VC_ANW + 8] = np.asarray(inp["L0_ffn_attn_norm_w"], f32).reshape(8, 128).T
    vecs[:, VC_MNW:VC_MNW + 8] = np.asarray(inp["L0_ffn_moe_norm_w"], f32).reshape(8, 128).T
    vecs[:, VC_BO:VC_BO + 8] = np.asarray(inp["L0_attn_bo"], f32).reshape(8, 128).T
    vecs[0:64, VC_BQ] = np.asarray(inp["L0_attn_bq"], f32)
    vecs[0:64, VC_BK] = np.asarray(inp["L0_attn_bk"], f32)
    vecs[0:64, VC_BV] = np.asarray(inp["L0_attn_bv"], f32)
    vecs[:, VC_EPSLN] = 1e-5
    vecs[:, VC_EPSRMS] = 1e-6
    com["vecs"] = vecs
    cst32 = np.zeros((128, 256), f32)
    cst32[:, 0:128] = 1.0
    cst32[:, 128:256] = np.eye(128, dtype=f32)
    com["cst32"] = cst32
    cst16 = np.zeros((128, 1280), f32)
    Rm = np.zeros((128, 128), f32)
    for hh in range(2):
        bb = hh * 64
        Rm[bb:bb + 32, bb + 32:bb + 64] = -np.eye(32)
        Rm[bb + 32:bb + 64, bb:bb + 32] = np.eye(32)
    cst16[:, 0:128] = Rm.T
    cst16[:, 128:256] = 1.0
    for ee in range(NEXP):
        cst16[ee, 256 + ee * 128: 256 + (ee + 1) * 128] = 1.0
    com["cst16"] = cst16.astype(bf)
    kk = np.arange(128)[:, None]
    cc = np.arange(1280)[None, :]
    com["maskbig"] = (cc >= kk + 384).astype(bf)
    inv_freq = 1.0 / (10000.0 ** (np.arange(0, HD, 2, dtype=f32) / HD))
    pos = np.arange(S, dtype=f32)
    freqs = pos[:, None] * inv_freq[None, :]
    emb = np.concatenate([freqs, freqs], axis=1)
    cos_t = np.cos(emb).T.astype(f32)
    sin_t = np.sin(emb).T.astype(f32)
    cossin = np.zeros((128, 4096), f32)
    cossin[0:64, 0:2048] = cos_t
    cossin[64:128, 0:2048] = cos_t
    cossin[0:64, 2048:4096] = sin_t
    cossin[64:128, 2048:4096] = sin_t
    com["cossin"] = cossin.astype(bf)
    return xT, com


def _launch(inp):
    xT, com = _host_inputs(inp)
    nc, ctx = build_nc()
    in_maps = []
    for c in range(8):
        m = dict(com)
        m["xT"] = np.ascontiguousarray(xT[c * NSEQ:(c + 1) * NSEQ])
        in_maps.append(m)
    return run_bass_kernel_spmd(nc, in_maps, core_ids=list(range(8)))


def kernel(**inp):
    res = _launch(inp)
    outs = [res.results[c]["out"] for c in range(8)]
    full = np.concatenate(outs, axis=0)
    return np.ascontiguousarray(full.transpose(0, 2, 1))


# version 15
# speedup vs baseline: 1.1096x; 1.1096x over previous
import contextlib
import numpy as np
import ml_dtypes
import concourse.bass as bass
import concourse.mybir as mybir
from concourse.bass_utils import run_bass_kernel_spmd

F32 = mybir.dt.float32
BF16 = mybir.dt.bfloat16
AF = mybir.ActivationFunctionType
ALU = mybir.AluOpType
AX = mybir.AxisListType

NSEQ = 4
S = 2048
D = 1024
DC = 8
NH = 16
HPAIRS = 8
HD = 64
NEXP = 8
HID = 4096
NT = 4
T5 = 512

DEBUG_TAPS: list = []
STOP_AFTER: str = ""


class Sem:
    __slots__ = ("name", "h", "count")

    def __init__(self, name):
        self.name = name
        self.h = None
        self.count = 0


class Ctx:
    def __init__(self, nc):
        self.nc = nc
        self.sems = []
        self.cur = None
        self.e = None
        self.waited = {}
        self.taps = {}
        self.g = {}
        self.stopped = False

    def sem(self, name):
        s = Sem(name)
        self.sems.append(s)
        return s

    def begin_pass(self, name, eng):
        self.cur = name
        self.e = eng
        self.waited = {}
        self.g = {}
        self.stopped = False
        for s in self.sems:
            s.count = 0

    def wait(self, eng, sem, val):
        if val <= 0:
            return
        k = (eng, sem.name)
        if self.waited.get(k, 0) >= val:
            return
        self.waited[k] = val
        if self.cur == eng:
            self.e.wait_ge(sem.h, val)

    def wait_ev(self, eng, ev):
        if ev is not None:
            self.wait(eng, ev[0], ev[1])

    def op(self, eng, emit, inc=None, n=1):
        if self.cur == eng:
            ins = emit(self.e)
            if inc is not None:
                ins.then_inc(inc.h, n)
        if inc is not None:
            inc.count += n
            return (inc, inc.count)
        return None

    def gw(self, eng, key):
        self.wait_ev(eng, self.g.get(key))

    def gs(self, key, ev):
        self.g[key] = ev

    def join(self, engines, sems):
        for eng in engines:
            for s in sems:
                self.wait(eng, s, s.count)


ENGS = ("pe", "act", "dve", "gp", "sp")

VC_ANW, VC_MNW, VC_BO = 0, 8, 16
VC_BQ, VC_BK, VC_BV = 24, 25, 26
VC_EPSLN, VC_EPSRMS = 27, 28

FA_SQ0 = 0
FA_SQ1 = 512
FA_BCA = 1024
FA_BCB = 1536
FA_BCC = 2048
FA_RT0 = 2048
FA_RT1 = 2056
FA_RT2 = 2064
FA_MX = 2560
FA_MB = 2624
FA_M1 = 2626
FA_M2 = 2627
FA_RW1 = 2628
FA_RW2 = 2629
FA_ROW0 = 3072
FA_ROW1 = 3584

BB_YB = 0
BB_QRAW = 0
BB_KRAW = 512
BB_EXP0 = 1024
BB_EXP1 = 1536
BB_WRT = 2048

CC_WBUF = 0
CC_QTR = 2048
CC_KTR = 4096
CC_AOP = 6144
CC_WVT = 8192


def build_nc():
    nc = bass.Bass("TRN2", target_bir_lowering=False)
    ctx = Ctx(nc)

    d = {}
    def din(name, shape, dtype):
        d[name] = nc.dram_tensor(name, shape, dtype, kind="ExternalInput").ap()
    din("xT", [NSEQ, D, S], F32)
    din("a_wq", [D, HD], F32)
    din("a_wk", [D, HD], F32)
    din("a_wv", [D, HD], F32)
    din("a_wo", [HD, D], F32)
    din("f_wqT", [D, D], F32)
    din("f_wkT", [D, D], F32)
    din("f_wvT", [D, D], F32)
    din("f_woT", [D, D], F32)
    din("gateT", [D, NEXP], F32)
    din("w1T", [NEXP, D, HID], F32)
    din("w2T", [NEXP, HID, D], F32)
    din("b1f", [128, NEXP * 32], F32)
    din("b2", [NEXP, D], F32)
    din("vecs", [128, 64], F32)
    din("cst32", [128, 256], F32)
    din("cst16", [128, 1280], BF16)
    din("maskbig", [128, 1280], BF16)
    din("cossin", [128, 4096], BF16)

    out = nc.dram_tensor("out", [NSEQ, D, S], F32, kind="ExternalOutput").ap()

    wS4 = nc.dram_tensor("wS4", [4 * D, D], BF16, kind="Internal").ap()
    wS1 = nc.dram_tensor("wS1", [NEXP, D, HID], BF16, kind="Internal").ap()
    wS2 = nc.dram_tensor("wS2", [NEXP, HID, D], BF16, kind="Internal").ap()
    yS = nc.dram_tensor("yS", [D, S], BF16, kind="Internal").ap()
    tnS = nc.dram_tensor("tnS", [D, S], BF16, kind="Internal").ap()
    aoS = nc.dram_tensor("aoS", [D, S], BF16, kind="Internal").ap()

    with contextlib.ExitStack() as stack:
        ec = stack.enter_context
        R = ec(nc.sbuf_tensor("R", [128, DC * S], F32))
        AF32 = ec(nc.sbuf_tensor("AF32", [128, 4096], F32))
        AB16 = ec(nc.sbuf_tensor("AB16", [128, 4096], BF16))
        A16a = ec(nc.sbuf_tensor("A16a", [128, 16384], BF16))
        A16b = ec(nc.sbuf_tensor("A16b", [128, 16384], BF16))
        A16c = ec(nc.sbuf_tensor("A16c", [128, 16384], BF16))
        cst32 = ec(nc.sbuf_tensor("cst32b", [128, 256], F32))
        cst16 = ec(nc.sbuf_tensor("cst16b", [128, 1280], BF16))
        maskb = ec(nc.sbuf_tensor("maskbH", [128, 1280], BF16))
        cossin = ec(nc.sbuf_tensor("cossinb", [128, 4096], BF16))
        aqkv = ec(nc.sbuf_tensor("aqkv", [128, 3 * 512], BF16))
        awo = ec(nc.sbuf_tensor("awo", [64, 1024], BF16))
        vecs = ec(nc.sbuf_tensor("vecsb", [128, 64], F32))
        b1b = ec(nc.sbuf_tensor("b1b", [128, 256], F32))
        b2b = ec(nc.sbuf_tensor("b2b", [8, 1024], BF16))
        gateb = ec(nc.sbuf_tensor("gateb", [128, 64], BF16))

        pp = [ec(nc.psum_tensor(f"pp{i}", [128, 512], F32)) for i in range(4)]
        avb = ec(nc.psum_tensor("avbp", [128, 512], F32))
        rowA = ec(nc.psum_tensor("rowAp", [128, 512], F32))
        rowB = ec(nc.psum_tensor("rowBp", [128, 512], F32))
        bcb = ec(nc.psum_tensor("bcbp", [128, 512], F32))

        sIn = ctx.sem("sIn")
        sPrep = ctx.sem("sPrep")
        sSt = ctx.sem("sSt")
        sW = [ctx.sem("sW0"), ctx.sem("sW1")]
        sPE = [ctx.sem("sPE0"), ctx.sem("sPE1")]
        sACT = [ctx.sem("sACT0"), ctx.sem("sACT1")]
        sDVE = [ctx.sem("sDVE0"), ctx.sem("sDVE1")]
        for s in ctx.sems:
            s.h = ec(nc.semaphore(s.name))

        T = dict(R=R, AF32=AF32, AB16=AB16, A16a=A16a, A16b=A16b, A16c=A16c,
                 cst32=cst32, cst16=cst16, maskb=maskb, cossin=cossin,
                 aqkv=aqkv, awo=awo, vecs=vecs, b1b=b1b, b2b=b2b, gateb=gateb,
                 pp=pp, avb=avb, rowA=rowA, rowB=rowB, bcb=bcb,
                 d=d, out=out, wS4=wS4, wS1=wS1, wS2=wS2, yS=yS, tnS=tnS,
                 aoS=aoS, sIn=sIn, sPrep=sPrep, sSt=sSt, sW=sW, sPE=sPE,
                 sACT=sACT, sDVE=sDVE, nc=nc)

        block = ec(nc.Block())

        def mk(engname):
            def body(eng):
                ctx.begin_pass(engname, eng)
                program(ctx, T)
            return body
        block.sync(mk("sp"))
        block.tensor(mk("pe"))
        block.scalar(mk("act"))
        block.vector(mk("dve"))
        block.gpsimd(mk("gp"))

    return nc, ctx



def tap(ctx, T, name, ap_fn, shape, np_dtype, ev_list):
    if name not in DEBUG_TAPS:
        return
    key = "dbg_" + name
    if key not in ctx.taps:
        dt = BF16 if np_dtype == ml_dtypes.bfloat16 else F32
        ctx.taps[key] = T["nc"].dram_tensor(key, shape, dt, kind="ExternalOutput").ap()
    dst = ctx.taps[key]
    for ev in ev_list:
        ctx.wait_ev("gp", ev)
    phase_join(ctx, T, engines=("gp",))
    ctx.op("gp", lambda e: e.dma_start(out=dst, in_=ap_fn()), inc=T["sSt"], n=16)


def stop_here(ctx, label):
    if STOP_AFTER and label == STOP_AFTER:
        ctx.stopped = True
    return ctx.stopped


def phase_join(ctx, T, engines=ENGS):
    comp = [T["sPE"][0], T["sPE"][1], T["sACT"][0], T["sACT"][1],
            T["sDVE"][0], T["sDVE"][1]]
    full = comp + [T["sW"][0], T["sW"][1], T["sIn"], T["sSt"]]
    for eng in engines:
        ctx.join([eng], comp if eng in ("pe", "act", "dve") else full)


def mm_chain(ctx, T, par, bank_key, psum_ap, pairs, tp=None):
    ctx.gw("pe", bank_key)
    n = len(pairs)
    ev = None
    for i, (lt, rh) in enumerate(pairs):
        kw = {"tile_position": tp} if tp is not None else {}
        inc_s = T["sPE"][par] if i == n - 1 else None
        ev = ctx.op("pe", lambda e, lt=lt, rh=rh, i=i, kw=kw, n=n: e.matmul(
            psum_ap, lt, rh, start=(i == 0), stop=(i == n - 1),
            skip_group_check=True, **kw), inc=inc_s)
    return ev



def program(ctx, T):
    prep(ctx, T)
    if stop_here(ctx, "prep"):
        finale(ctx, T)
        return
    for seq in range(NSEQ):
        ev_r = load_x(ctx, T, seq)
        done = False
        for call in (0, 1):
            lbl = f"s{seq}c{call}"
            ev_r, ev_tn1 = phaseA(ctx, T, seq, call, ev_r)
            if stop_here(ctx, "A" + lbl):
                done = True
                break
            ev_r = phaseB(ctx, T, seq, call, ev_r, ev_tn1)
            if stop_here(ctx, "B" + lbl):
                done = True
                break
            ev_r = phaseC(ctx, T, seq, call, ev_r)
            if stop_here(ctx, "C" + lbl):
                done = True
                break
            if call == 0:
                ev_r = phase_mid(ctx, T, seq, ev_r)
                if stop_here(ctx, "M" + lbl):
                    done = True
                    break
        store_out(ctx, T, seq, ev_r)
        if done:
            break
    finale(ctx, T)


def prep(ctx, T):
    d = T["d"]
    sIn, sPrep = T["sIn"], T["sPrep"]
    for dst, src in [
        (T["cst32"][:, :], d["cst32"]),
        (T["cst16"][:, :], d["cst16"]),
        (T["maskb"][:, :], d["maskbig"]),
        (T["cossin"][:, :], d["cossin"]),
        (T["vecs"][:, :], d["vecs"]),
        (T["b1b"][:, :], d["b1f"]),
    ]:
        ctx.op("sp", lambda e, dst=dst, src=src: e.dma_start(out=dst, in_=src),
               inc=sIn, n=16)
    gtmp = T["AF32"][:, FA_SQ0:FA_SQ0 + 64]
    ctx.op("sp", lambda e: e.dma_start(
        out=gtmp.rearrange("p (c e) -> p c e", c=DC),
        in_=d["gateT"].rearrange("(c p) e -> p c e", p=128)), inc=sIn, n=16)
    ctx.wait("gp", sIn, sIn.count)
    ev = ctx.op("gp", lambda e: e.tensor_copy(out=T["gateb"][:, :], in_=gtmp),
                inc=sPrep)
    ctx.gs("sq0", ev)
    for i, nm in enumerate(["a_wq", "a_wk", "a_wv"]):
        ctx.op("gp", lambda e, i=i, nm=nm: e.dma_start(
            out=T["aqkv"][:, i * 512:(i + 1) * 512].rearrange("p (c o) -> p c o", c=DC),
            in_=d[nm].rearrange("(c p) o -> p c o", p=128)), inc=sPrep, n=16)
    ctx.op("gp", lambda e: e.dma_start(out=T["awo"][:, :], in_=d["a_wo"]), inc=sPrep, n=16)
    ctx.op("gp", lambda e: e.dma_start(out=T["b2b"][:, :], in_=d["b2"]), inc=sPrep, n=16)
    ctx.g["prep_attn"] = (sPrep, sPrep.count)
    ev = None
    for i, nm in enumerate(["f_wqT", "f_wkT", "f_wvT", "f_woT"]):
        ev = ctx.op("gp", lambda e, i=i, nm=nm: e.dma_start(
            out=T["wS4"][i * D:(i + 1) * D, :], in_=d[nm]), inc=sPrep, n=16)
    ctx.g["prep_ws4"] = ev
    for ee in range(NEXP):
        ctx.op("gp", lambda e, ee=ee: e.dma_start(
            out=T["wS1"][ee], in_=d["w1T"][ee]), inc=sPrep, n=16)
        ev = ctx.op("gp", lambda e, ee=ee: e.dma_start(
            out=T["wS2"][ee], in_=d["w2T"][ee]), inc=sPrep, n=16)
    ctx.g["prep_all"] = ev


def load_x(ctx, T, seq):
    ctx.wait("sp", T["sSt"], T["sSt"].count)
    return ctx.op("sp", lambda e: e.dma_start(
        out=T["R"].rearrange("p (c t) -> p c t", c=DC),
        in_=T["d"]["xT"][seq].rearrange("(c p) t -> p c t", p=128)), inc=T["sIn"], n=16)


def store_out(ctx, T, seq, ev_r):
    ctx.wait_ev("gp", ev_r)
    phase_join(ctx, T, engines=("gp",))
    ctx.op("gp", lambda e: e.dma_start(
        out=T["out"][seq].rearrange("(c p) t -> p c t", p=128),
        in_=T["R"].rearrange("p (c t) -> p c t", c=DC)), inc=T["sSt"], n=16)


def finale(ctx, T):
    ctx.wait("gp", T["sSt"], T["sSt"].count)
    ctx.wait("sp", T["sSt"], T["sSt"].count)



def _rows_rstd(ctx, T, par, eps_col, with_mean=False, s1b=None, s2b=None,
               r0c=FA_ROW0, r1c=FA_ROW1, msqc=FA_SQ0, rkey="rows",
               slkey="rows_sl"):
    AF32, vecs = T["AF32"], T["vecs"]
    sDVE, sACT = T["sDVE"][par], T["sACT"][par]
    s1b = s1b if s1b is not None else T["rowA"]
    s2b = s2b if s2b is not None else T["rowB"]
    r0 = AF32[0:1, r0c:r0c + 512]
    r1 = AF32[0:1, r1c:r1c + 512]
    msq = AF32[0:1, msqc:msqc + 512]
    invD = 1.0 / D
    ctx.gw("dve", slkey)
    if with_mean:
        ctx.op("dve", lambda e: e.tensor_scalar_mul(out=r0, in0=s1b[0:1, :],
                                                    scalar1=invD))
        ctx.op("dve", lambda e: e.tensor_scalar_mul(out=r1, in0=s2b[0:1, :],
                                                    scalar1=invD))
        ev = ctx.op("dve", lambda e: e.tensor_mul(out=msq, in0=r0, in1=r0),
                    inc=sDVE)
        ctx.gs(rkey, ev)
        ev = ctx.op("dve", lambda e: e.tensor_sub(out=r1, in0=r1, in1=msq), inc=sDVE)
        ctx.wait_ev("act", ev)
        ctx.op("act", lambda e: e.activation(
            out=r1, in_=r1, func=AF.Ln, bias=vecs[0:1, eps_col:eps_col + 1]))
        ev = ctx.op("act", lambda e: e.activation(
            out=r1, in_=r1, func=AF.Exp, scale=-0.5), inc=sACT)
        ctx.wait_ev("dve", ev)
        ev = ctx.op("dve", lambda e: e.scalar_tensor_tensor(
            out=r0, in0=r0, scalar=-1.0, in1=r1, op0=ALU.mult, op1=ALU.mult), inc=sDVE)
    else:
        ev = ctx.op("dve", lambda e: e.tensor_scalar_mul(
            out=r1, in0=s2b[0:1, :], scalar1=invD), inc=sDVE)
        ctx.gs(rkey, ev)
        ctx.wait_ev("act", ev)
        ctx.op("act", lambda e: e.activation(
            out=r1, in_=r1, func=AF.Ln, bias=vecs[0:1, eps_col:eps_col + 1]))
        ev = ctx.op("act", lambda e: e.activation(
            out=r1, in_=r1, func=AF.Exp, scale=-0.5), inc=sACT)
    return ev


def _bcast_row(ctx, T, par, row_ap, evac_to=None, evac_key=None, bank=None,
               bkey="bcb"):
    ones_row = T["cst32"][0:1, 0:128]
    bank = bank if bank is not None else T["bcb"]
    ctx.gw("pe", bkey)
    ev = ctx.op("pe", lambda e: e.matmul(bank[:, :], ones_row, row_ap,
                                         start=True, stop=True, skip_group_check=True),
                inc=T["sPE"][par])
    if evac_to is None:
        return ev, bank[:, :]
    ctx.wait_ev("act", ev)
    if evac_key is not None:
        ctx.gw("act", evac_key)
    ev = ctx.op("act", lambda e: e.copy(out=evac_to, in_=bank[:, :]),
                inc=T["sACT"][par])
    ctx.gs(bkey, ev)
    return ev, evac_to


def _ln_stats(ctx, T, par, src_tiles, with_mean, eps_col, s1b=None, s2b=None,
              s1key="rows", s2key="rowsB", r0c=FA_ROW0, r1c=FA_ROW1,
              msqc=FA_SQ0, slkey="rows_sl", sqa=None, sqkeys=("sqA0", "sqA1")):
    AF32, cst32 = T["AF32"], T["cst32"]
    sPE, sACT = T["sPE"][par], T["sACT"][par]
    ones_col = cst32[:, 0:1]
    ones_bf = T["cst16"][:, 128:129]
    s1b = s1b if s1b is not None else T["rowA"]
    s2b = s2b if s2b is not None else T["rowB"]
    if sqa is None:
        sqa = T["A16c"][:, 0:1024]
    ev_s2 = None
    pend = []

    def emit_s2(dc, ev_sq, sq, key):
        nonlocal ev_s2
        ctx.wait_ev("pe", ev_sq)
        if dc == 0:
            ctx.gw("pe", s2key)
        ev_s2 = ctx.op("pe", lambda e, dc=dc, sq=sq: e.matmul(
            s2b[0:1, :], ones_bf, sq, start=(dc == 0), stop=(dc == DC - 1),
            skip_group_check=True), inc=sPE)
        ctx.gs(key, ev_s2)

    for dc in range(DC):
        sq = sqa[:, (dc % 2) * 512:(dc % 2) * 512 + 512]
        key = sqkeys[dc % 2]
        src = src_tiles(dc)
        ctx.gw("act", key)
        ev_sq = ctx.op("act", lambda e, src=src, sq=sq: e.activation(
            out=sq, in_=src, func=AF.Square), inc=sACT)
        if dc == 0:
            ctx.gw("pe", s1key)
        if with_mean:
            ctx.op("pe", lambda e, src=src, dc=dc: e.matmul(
                s1b[0:1, :], ones_col, src, start=(dc == 0), stop=(dc == DC - 1),
                skip_group_check=True))
        pend.append((dc, ev_sq, sq, key))
        if len(pend) == 2:
            emit_s2(*pend.pop(0))
    emit_s2(*pend.pop(0))
    ctx.wait_ev("dve", ev_s2)
    ev = _rows_rstd(ctx, T, par, eps_col, with_mean=with_mean, s1b=s1b,
                    s2b=s2b, r0c=r0c, r1c=r1c, msqc=msqc, rkey=s1key,
                    slkey=slkey)
    ctx.gs(s2key, ev)
    return ev


def phaseA(ctx, T, seq, call, ev_r):
    par = call
    phase_join(ctx, T)
    for eng in ("pe", "act", "dve"):
        ctx.wait_ev(eng, ev_r)
    R, AF32, AB16, A16a = T["R"], T["AF32"], T["AB16"], T["A16a"]
    vecs = T["vecs"]
    sDVE, sSt = T["sDVE"][par], T["sSt"]
    S1B = [T["rowA"], T["pp"][2]]
    S2B = [T["rowB"], T["pp"][3]]
    S1K = ["rows", "pp2"]
    S2K = ["rowsB", "pp3"]
    R0C = [FA_ROW0, FA_BCC]
    R1C = [FA_ROW1, 2560]
    BCB = [T["bcb"], T["avb"]]
    BCK = ["bcb", "avb"]
    BCAC = [FA_BCA, FA_BCB]
    BAK = ["bca", "bcb2"]
    SLK = ["rows_sl", "rows_sl2"]
    ev_fold = None
    ev_tn1 = None
    A16b, A16c = T["A16b"], T["A16c"]
    sW = T["sW"][par]
    ctx.wait_ev("sp", ctx.g.get("prep_ws4"))
    ctx.op("sp", lambda e: e.dma_start(
        out=A16c[:, CC_WVT:CC_WVT + 8192].rearrange("p (c o) -> p c o", c=DC),
        in_=T["wS4"][2 * D:3 * D].rearrange("(c p) o -> p c o", p=128)),
        inc=sW, n=16)
    ev_wv = (sW, sW.count)

    tn1_evs = [None] * NT

    def vbuild(t):
        ctx.wait_ev("pe", ev_wv)
        ctx.wait_ev("pe", tn1_evs[t])
        for tc in range(4 * t, 4 * t + 4):
            for half in range(2):
                bk = half
                pairs = [(A16a[:, dc * S + tc * 128: dc * S + tc * 128 + 128],
                          A16c[:, CC_WVT + dc * 1024 + half * 512:
                               CC_WVT + dc * 1024 + half * 512 + 512])
                         for dc in range(DC)]
                ev = mm_chain(ctx, T, par, f"pp{bk}", T["pp"][bk][:, :], pairs)
                ctx.wait_ev("act", ev)
                ev_v = ctx.op("act", lambda e, bk=bk, tc=tc, half=half: e.copy(
                    out=A16b[:, tc * 1024 + half * 512:
                             tc * 1024 + half * 512 + 512],
                    in_=T["pp"][bk][:, :]), inc=T["sACT"][par])
                ctx.gs(f"pp{bk}", ev_v)

    def rtile(t, dc):
        return R[:, dc * S + t * T5: dc * S + t * T5 + T5]

    def ytile(dc):
        return AB16[:, BB_YB + dc * 512: BB_YB + dc * 512 + 512]

    def ln_rows(t):
        p = t % 2
        ev = _ln_stats(ctx, T, par, lambda dc, t=t: rtile(t, dc), True, VC_EPSLN,
                       s1b=S1B[p], s2b=S2B[p], s1key=S1K[p], s2key=S2K[p],
                       r0c=R0C[p], r1c=R1C[p], msqc=FA_SQ0, slkey=SLK[p])
        if t >= 2:
            vbuild(t - 2)
        ctx.wait_ev("pe", ev)
        evA, bcA = _bcast_row(ctx, T, par, AF32[0:1, R1C[p]:R1C[p] + 512],
                              evac_to=AF32[:, BCAC[p]:BCAC[p] + 512],
                              evac_key=BAK[p], bank=BCB[p], bkey=BCK[p])
        evB, bcB = _bcast_row(ctx, T, par, AF32[0:1, R0C[p]:R0C[p] + 512],
                              bank=BCB[p], bkey=BCK[p])
        ctx.gs(SLK[p], evB)
        return evA, bcA, evB, bcB

    def y_and_tn(t, evA, bcA, evB, bcB):
        nonlocal ev_fold, ev_tn1
        p = t % 2
        ctx.wait_ev("dve", evA)
        ctx.wait_ev("dve", evB)
        ytmp = AF32[:, FA_SQ1:FA_SQ1 + 512]
        ctx.gw("dve", "sq1")
        ev_y = None
        for dc in range(DC):
            rsl = rtile(t, dc)
            ybd = ytile(dc)
            ctx.op("dve", lambda e, rsl=rsl, ytmp=ytmp, bcA=bcA: e.tensor_mul(
                out=ytmp, in0=rsl, in1=bcA))
            ev_y = ctx.op("dve", lambda e, ybd=ybd, ytmp=ytmp, bcB=bcB: e.tensor_add(
                out=ybd, in0=ytmp, in1=bcB), inc=sDVE)
            ev_fold = ctx.op("dve", lambda e, rsl=rsl, ybd=ybd: e.scalar_tensor_tensor(
                out=rsl, in0=ybd, scalar=0.5, in1=rsl, op0=ALU.mult, op1=ALU.add),
                inc=sDVE)
        ctx.gs(BCK[p], ev_y)
        ctx.gs(BAK[p], ev_y)
        ctx.gs("sq1", ev_fold)
        ctx.wait_ev("act", ev_fold)
        ev = _ln_stats(ctx, T, par, lambda dc: ytile(dc), False, VC_EPSRMS,
                       s1b=S1B[p], s2b=S2B[p], s1key=S1K[p], s2key=S2K[p],
                       r0c=R0C[p], r1c=R1C[p], msqc=FA_SQ0, slkey=SLK[p])
        ctx.wait_ev("pe", ev)
        evA2, bcA2 = _bcast_row(ctx, T, par, AF32[0:1, R1C[p]:R1C[p] + 512],
                                evac_to=AF32[:, BCAC[p]:BCAC[p] + 512],
                                evac_key=BAK[p], bank=BCB[p], bkey=BCK[p])
        ctx.gs(SLK[p], evA2)
        ctx.wait_ev("dve", evA2)
        ttmp = AF32[:, FA_SQ1:FA_SQ1 + 512]
        ctx.gw("dve", "sq1")
        for dc in range(DC):
            ybd = ytile(dc)
            tnd = A16a[:, dc * S + t * T5: dc * S + t * T5 + T5]
            ctx.op("dve", lambda e, ybd=ybd, ttmp=ttmp, bcA2=bcA2: e.tensor_mul(
                out=ttmp, in0=ybd, in1=bcA2))
            ev_tn1 = ctx.op("dve", lambda e, tnd=tnd, dc=dc, ttmp=ttmp:
                            e.tensor_scalar_mul(
                                out=tnd, in0=ttmp,
                                scalar1=vecs[:, VC_ANW + dc:VC_ANW + dc + 1]),
                            inc=sDVE)
        ctx.gs("sq1", ev_tn1)
        ctx.gs(BAK[p], ev_tn1)
        tn1_evs[t] = ev_tn1
        ctx.wait_ev("sp", ev_tn1)
        ctx.op("sp", lambda e, t=t: e.dma_start(
            out=T["yS"].rearrange("(c p) t -> p c t", p=128)[:, :, t * T5:t * T5 + T5],
            in_=AB16[:, BB_YB:BB_YB + 4096].rearrange("p (c t) -> p c t", c=DC)),
            inc=sSt, n=16)
        ctx.wait("dve", sSt, sSt.count)

    pend = None
    for t in range(NT):
        got = ln_rows(t)
        if pend is not None:
            y_and_tn(pend[0], *pend[1])
        pend = (t, got)
    y_and_tn(pend[0], *pend[1])
    vbuild(NT - 2)
    vbuild(NT - 1)
    tap(ctx, T, f"A_R_{seq}_{call}", lambda: T["R"][:, :], [128, DC * S],
        np.float32, [ev_fold])
    tap(ctx, T, f"A_TN1_{seq}_{call}", lambda: A16a[:, :], [128, 16384],
        ml_dtypes.bfloat16, [ev_tn1])
    return ev_fold, ev_tn1


def _softmax_pair(ctx, T, par, heads, kT_fn, qT_fn, v_fn, ao_fn,
                  qfull=None, kfull=None, sqarea=None, bias_col=None):
    AF32, AB16, maskb, cst16 = T["AF32"], T["AB16"], T["maskb"], T["cst16"]
    sPE, sACT, sDVE = T["sPE"][par], T["sACT"][par], T["sDVE"][par]
    ones_bf = cst16[:, 128:129]
    rot = [1, 2, 3]
    mxbase = {0: FA_MX, 64: FA_BCC}
    m1col = {0: FA_M1, 64: FA_M2}
    mbcol = {0: FA_MB, 64: FA_MB + 1}
    mbkey = {0: "mb0", 64: "mb1"}
    ebufs = {0: [(BB_EXP0, "exp0"), (BB_EXP1, "exp1")],
             64: [(BB_QRAW, "qraw"), (BB_KRAW, "kraw")]}
    dnbank = {0: T["rowA"], 64: T["rowB"]}
    dnkey = {0: "rows", 64: "rowsB"}
    avbank = {0: T["pp"][0], 64: T["avb"]}
    avkey = {0: "pp0", 64: "avb"}
    rowsl = {0: FA_ROW0, 64: FA_ROW1}
    bcsl = {0: (FA_BCA, "bca"), 64: (FA_BCB, "bcb2")}

    hi = heads[-1] + 64
    sqq = sqarea[0:hi, 0:2048]
    sqk = sqarea[0:hi, 2048:4096]
    ctx.gw("act", "sqs0")
    ev_sq = ctx.op("act", lambda e: e.activation(out=sqq, in_=qfull,
                                                 func=AF.Square), inc=sACT)
    ctx.gw("act", "sqs1")
    ev_sk = ctx.op("act", lambda e: e.activation(out=sqk, in_=kfull,
                                                 func=AF.Square), inc=sACT)
    nbanks = [T["rowA"], T["rowB"], T["pp"][0], T["avb"]]
    nkeys = ["rows", "rowsB", "pp0", "avb"]
    ones_bf = cst16[:, 128:129]
    ev = None
    for h in heads:
        for mi, (sq, evs) in enumerate(((sqq, ev_sq), (sqk, ev_sk))):
            ctx.wait_ev("pe", evs)
            for sl in range(4):
                ctx.gw("pe", nkeys[sl])
                ev = ctx.op("pe", lambda e, h=h, sq=sq, sl=sl: e.matmul(
                    nbanks[sl][0:1, :], cst16[h:h + 64, 128:129],
                    sq[h:h + 64, sl * 512:sl * 512 + 512], start=True, stop=True,
                    skip_group_check=True, tile_position=(h, 0)), inc=sPE)
            ctx.wait_ev("dve", ev)
            nc_ = mxbase[h] + mi * 8
            for sl in range(4):
                ev = ctx.op("dve", lambda e, sl=sl, nc_=nc_: e.reduce_max(
                    out=AF32[0:1, nc_ + sl:nc_ + sl + 1],
                    in_=nbanks[sl][0:1, :], axis=AX.X), inc=sDVE)
                ctx.gs(nkeys[sl], ev)
            ctx.wait_ev("dve", ev)
            ev = ctx.op("dve", lambda e, nc_=nc_: e.reduce_max(
                out=AF32[0:1, nc_ + 4:nc_ + 5], in_=AF32[0:1, nc_:nc_ + 4],
                axis=AX.X), inc=sDVE)
            ctx.wait_ev("dve", ev)
    ctx.gs("sqs0", ev)
    ctx.gs("sqs1", ev)
    for h in heads:
        mb = mxbase[h]
        mc = m1col[h]
        ev = ctx.op("dve", lambda e, mb=mb, mc=mc: e.tensor_mul(
            out=AF32[0:1, mc:mc + 1], in0=AF32[0:1, mb + 4:mb + 5],
            in1=AF32[0:1, mb + 12:mb + 13]), inc=sDVE)
        ctx.wait_ev("act", ev)
        ev = ctx.op("act", lambda e, mc=mc: e.activation(
            out=AF32[0:1, mc:mc + 1], in_=AF32[0:1, mc:mc + 1], func=AF.Sqrt),
            inc=sACT)
        ctx.wait_ev("dve", ev)
        ev = ctx.op("dve", lambda e, mc=mc: e.tensor_scalar_mul(
            out=AF32[0:1, mc:mc + 1], in0=AF32[0:1, mc:mc + 1], scalar1=-0.125),
            inc=sDVE)
        ctx.wait_ev("pe", ev)
        ctx.gw("pe", "bcb")
        ev = ctx.op("pe", lambda e, mc=mc: e.matmul(
            T["bcb"][:, 0:1], T["cst32"][0:1, 0:128], AF32[0:1, mc:mc + 1],
            start=True, stop=True, skip_group_check=True), inc=sPE)
        ctx.wait_ev("dve", ev)
        ctx.gw("dve", mbkey[h])
        ev_mb = ctx.op("dve", lambda e, h=h: e.tensor_copy(
            out=AF32[:, mbcol[h]:mbcol[h] + 1], in_=T["bcb"][:, 0:1]), inc=sDVE)
        ctx.gs("bcb", ev_mb)
        ctx.g["_mbev" + str(h)] = ev_mb

    ev_ao = None
    last_exp = None
    for qs in range(NT):
        nkc = 4 * qs + 4
        ev_e = {h: [None] * nkc for h in heads}

        def tile_off(kc, qs=qs):
            return max(0, (kc - 4 * qs) * 128)

        def emit_sT(h, kc, qs=qs):
            off = tile_off(kc)
            w = 512 - off
            bk = rot[(kc * len(heads) + heads.index(h)) % 3]
            return bk, mm_chain(ctx, T, par, f"pp{bk}", T["pp"][bk][:, 0:w],
                                [(kT_fn(h, kc), qT_fn(h, qs)[:, off:512])],
                                tp=(h, 0))

        def emit_exp(h, kc, bk, ev_s, qs=qs):
            off = tile_off(kc)
            w = 512 - off
            boff, ekey = ebufs[h][kc % 2]
            eb = AB16[:, boff:boff + w]
            ctx.wait_ev("act", ev_s)
            ctx.wait_ev("act", ctx.g["_mbev" + str(h)])
            ctx.gw("act", ekey)
            eve = ctx.op("act", lambda e, bk=bk, eb=eb, h=h, w=w: e.activation(
                out=eb, in_=T["pp"][bk][:, 0:w], func=AF.Exp, scale=0.125,
                bias=AF32[:, mbcol[h]:mbcol[h] + 1]), inc=sACT)
            ctx.gs(f"pp{bk}", eve)
            if kc >= 4 * qs:
                ctx.wait_ev("dve", eve)
                eve = ctx.op("dve", lambda e, eb=eb, w=w: e.tensor_mul(
                    out=eb, in0=eb, in1=maskb[:, 384:384 + w]), inc=sDVE)
            return eve

        def emit_cons(h, kc, qs=qs, nkc=nkc):
            off = tile_off(kc)
            w = 512 - off
            boff, ekey = ebufs[h][kc % 2]
            eb = AB16[:, boff:boff + w]
            if kc >= 4 * qs:
                ctx.wait("pe", sDVE, ev_e[h][kc][1])
            else:
                ctx.wait("pe", sACT, ev_e[h][kc][1])
            if kc == 0:
                ctx.gw("pe", dnkey[h])
                ctx.gw("pe", avkey[h])
            ctx.op("pe", lambda e, eb=eb, kc=kc, h=h, off=off: e.matmul(
                dnbank[h][0:1, off:512], ones_bf, eb, start=(kc == 0),
                stop=(kc == nkc - 1), skip_group_check=True))
            ev = ctx.op("pe", lambda e, eb=eb, kc=kc, h=h, off=off: e.matmul(
                avbank[h][h:h + 64, off:512], v_fn(h, kc), eb,
                start=(kc == 0), stop=(kc == nkc - 1), skip_group_check=True,
                tile_position=(0, h)), inc=sPE)
            ctx.gs(ekey, ev)
            return ev

        ev_av = {}
        for kc in range(nkc):
            sts = [(h,) + emit_sT(h, kc) for h in heads]
            for h, bk, ev_s in sts:
                ev_e[h][kc] = emit_exp(h, kc, bk, ev_s)
            if kc >= 1:
                for h in heads:
                    ev_av[h] = emit_cons(h, kc - 1)
        for h in heads:
            ev_av[h] = emit_cons(h, nkc - 1)
            last_exp = ev_e[h][nkc - 1]
        for h in heads:
            rs = rowsl[h]
            ctx.wait_ev("act", ev_av[h])
            ctx.gw("act", "rows_sl" if h == 0 else "rows_sl2")
            ctx.op("act", lambda e, h=h, rs=rs: e.activation(
                out=AF32[0:1, rs:rs + 512], in_=dnbank[h][0:1, :], func=AF.Ln))
            ev = ctx.op("act", lambda e, rs=rs: e.activation(
                out=AF32[0:1, rs:rs + 512], in_=AF32[0:1, rs:rs + 512],
                func=AF.Exp, scale=-1.0), inc=sACT)
            ctx.gs(dnkey[h], ev)
            ctx.g["_dn" + str(h)] = ev
        for h in heads:
            rs = rowsl[h]
            dst, dkey = bcsl[h]
            ctx.wait_ev("pe", ctx.g["_dn" + str(h)])
            evb, bcR = _bcast_row(ctx, T, par, AF32[0:1, rs:rs + 512],
                                  evac_to=AF32[:, dst:dst + 512], evac_key=dkey)
            ctx.gs("rows_sl" if h == 0 else "rows_sl2", evb)
            ctx.g["_bc" + str(h)] = (evb, bcR)
        for h in heads:
            evb, bcR = ctx.g["_bc" + str(h)]
            dst, dkey = bcsl[h]
            ctx.wait_ev("dve", evb)
            aop = ao_fn(h, qs)
            if bias_col is None:
                ev_ao = ctx.op("dve", lambda e, aop=aop, h=h, bcR=bcR: e.tensor_mul(
                    out=aop, in0=avbank[h][h:h + 64, :], in1=bcR[h:h + 64, :]),
                    inc=sDVE)
            else:
                ctx.op("dve", lambda e, aop=aop, h=h, bcR=bcR: e.tensor_mul(
                    out=aop, in0=avbank[h][h:h + 64, :], in1=bcR[h:h + 64, :]))
                ev_ao = ctx.op("dve", lambda e, aop=aop: e.tensor_scalar_add(
                    out=aop, in0=aop,
                    scalar1=T["vecs"][0:64, bias_col:bias_col + 1]), inc=sDVE)
            ctx.gs(avkey[h], ev_ao)
            ctx.gs(dkey, ev_ao)
    for h in heads:
        ctx.gs(mbkey[h], last_exp)
    return ev_ao


def phaseB(ctx, T, seq, call, ev_r, ev_tn1):
    par = call
    phase_join(ctx, T)
    R, AF32, AB16 = T["R"], T["AF32"], T["AB16"]
    A16a, A16b, A16c = T["A16a"], T["A16b"], T["A16c"]
    cst16, cossin, vecs = T["cst16"], T["cossin"], T["vecs"]
    sPE, sACT, sDVE = T["sPE"][par], T["sACT"][par], T["sDVE"][par]
    sW, sSt = T["sW"][par], T["sSt"]
    WQ, WK, WV, WO = 0, D, 2 * D, 3 * D


    rot = cst16[:, 0:128]
    qTr = A16c[:, CC_QTR:CC_QTR + 2048]
    kTr = A16c[:, CC_KTR:CC_KTR + 2048]
    aopair = A16c[:, CC_AOP:CC_AOP + 2048]
    wbuf = A16c[:, CC_WBUF:CC_WBUF + 2048]
    ev_proj = None
    for hp in range(HPAIRS):
        ctx.gw("sp", "wbuf")
        for i, woff in enumerate([WQ, WK]):
            ctx.op("sp", lambda e, woff=woff, i=i, hp=hp: e.dma_start(
                out=wbuf[:, i * 1024:(i + 1) * 1024].rearrange("p (c o) -> p c o", c=DC),
                in_=T["wS4"][woff:woff + D].rearrange("(c p) o -> p c o", p=128)
                [:, :, hp * 128:(hp + 1) * 128]), inc=sW, n=16)
        ctx.wait("pe", sW, sW.count)
        for t in range(NT):
            ts = t * T5
            for which in range(2):
                bk = which
                dst = qTr if which == 0 else kTr
                rkey = "qraw" if which == 0 else "kraw"
                raw = AB16[:, (BB_QRAW if which == 0 else BB_KRAW):][:, 0:512]
                ctx.gw("dve", "qkTr")
                pairs = [(wbuf[:, which * 1024 + dc * 128: which * 1024 + dc * 128 + 128],
                          A16a[:, dc * S + ts: dc * S + ts + T5]) for dc in range(DC)]
                ev = mm_chain(ctx, T, par, f"pp{bk}", T["pp"][bk][:, :], pairs)
                ctx.wait_ev("act", ev)
                ctx.gw("act", rkey)
                ev_raw = ctx.op("act", lambda e, bk=bk, raw=raw: e.copy(
                    out=raw, in_=T["pp"][bk][:, :]), inc=sACT)
                ctx.gs(f"pp{bk}", ev_raw)
                ctx.wait_ev("pe", ev_raw)
                ctx.gw("pe", "bcb")
                ev_rot = ctx.op("pe", lambda e, raw=raw: e.matmul(
                    T["bcb"][:, :], rot, raw, start=True, stop=True,
                    skip_group_check=True), inc=sPE)
                ctx.wait_ev("dve", ev_rot)
                cosb = cossin[:, ts:ts + T5]
                sinb = cossin[:, 2048 + ts:2048 + ts + T5]
                ctx.gw("dve", "sq1")
                ctx.op("dve", lambda e, raw=raw, cosb=cosb: e.tensor_mul(
                    out=AF32[:, FA_SQ1:FA_SQ1 + 512], in0=raw, in1=cosb))
                ctx.gw("dve", "bcc")
                ctx.op("dve", lambda e, sinb=sinb: e.tensor_mul(
                    out=AF32[:, FA_BCC:FA_BCC + 512], in0=T["bcb"][:, :], in1=sinb))
                ev_proj = ctx.op("dve", lambda e, dst=dst, ts=ts: e.tensor_add(
                    out=dst[:, ts:ts + T5], in0=AF32[:, FA_SQ1:FA_SQ1 + 512],
                    in1=AF32[:, FA_BCC:FA_BCC + 512]), inc=sDVE)
                ctx.gs("bcb", ev_proj)
                ctx.gs(rkey, ev_proj)
                ctx.gs("sq1", ev_proj)
                ctx.gs("bcc", ev_proj)
        ctx.gs("wbuf", (sPE, sPE.count))
        ctx.wait_ev("pe", ev_proj)
        ctx.gw("dve", "aopair")
        ev_ao = _softmax_pair(
            ctx, T, par, [0, 64],
            kT_fn=lambda h, kc: kTr[h:h + 64, kc * 128:kc * 128 + 128],
            qT_fn=lambda h, qs: qTr[h:h + 64, qs * T5:qs * T5 + T5],
            v_fn=lambda h, kc, hp=hp: A16b[:, kc * 1024 + (2 * hp) * 64 + h:
                                           kc * 1024 + (2 * hp) * 64 + h + 64],
            ao_fn=lambda h, qs: aopair[h:h + 64, qs * T5:qs * T5 + T5],
            qfull=qTr, kfull=kTr, sqarea=A16c[:, 8192:12288])
        ctx.gs("qkTr", (sPE, sPE.count))
        ctx.wait_ev("gp", ev_ao)
        ev_sp = ctx.op("gp", lambda e, hp=hp: e.dma_start(
            out=T["aoS"][hp * 128:(hp + 1) * 128, :], in_=aopair), inc=sSt, n=16)
        ctx.gs("aopair", ev_sp)

    tap(ctx, T, f"B_ao_{seq}_{call}", lambda: T["aoS"][:, :], [D, S],
        ml_dtypes.bfloat16, [])
    ctx.wait("sp", sPE, sPE.count)
    ctx.op("sp", lambda e: e.dma_start(
        out=A16b[:, 0:8192].rearrange("p (c o) -> p c o", c=DC),
        in_=T["wS4"][WO:WO + D].rearrange("(c p) o -> p c o", p=128)), inc=sW, n=16)
    aot = A16c[:, 0:4096]
    yt = A16c[:, 4096:8192]
    htile = A16a[:, 0:4096]
    tn2t = A16a[:, 4096:8192]
    ones_col = T["cst32"][:, 0:1]
    ev_rupd = None
    ev_wr = None
    def emit_routing(t):
        nonlocal ev_wr
        ts = t * T5
        ctx.wait_ev("pe", ev_tn2)
        for tsub in range(4):
            off = tsub * 128
            pairs = [(tn2t[:, dc * 512 + off: dc * 512 + off + 128],
                      T["gateb"][:, dc * 8: dc * 8 + 8]) for dc in range(DC)]
            ev_lg = mm_chain(ctx, T, par, "avb", T["avb"][:, 0:8], pairs)
            lg = T["avb"][:, 0:8]
            ctx.wait_ev("dve", ev_lg)
            ctx.gw("dve", "rt0")
            def dsync(emit):
                ev = ctx.op("dve", emit, inc=sDVE)
                ctx.wait_ev("dve", ev)
                return ev
            dsync(lambda e, lg=lg: e.reduce_max(
                out=AF32[:, FA_M1:FA_M1 + 1], in_=lg, axis=AX.X))
            dsync(lambda e, lg=lg: e.tensor_scalar(
                out=AF32[:, FA_RT0:FA_RT0 + 8], in0=lg,
                scalar1=AF32[:, FA_M1:FA_M1 + 1], scalar2=1.0, op0=ALU.is_ge,
                op1=ALU.mult))
            ev = dsync(lambda e, lg=lg: e.scalar_tensor_tensor(
                out=AF32[:, FA_RT1:FA_RT1 + 8], in0=AF32[:, FA_RT0:FA_RT0 + 8],
                scalar=-1e30, in1=lg, op0=ALU.mult, op1=ALU.add))
            ctx.gs("avb", ev)
            dsync(lambda e: e.reduce_max(
                out=AF32[:, FA_M2:FA_M2 + 1], in_=AF32[:, FA_RT1:FA_RT1 + 8],
                axis=AX.X))
            ev = dsync(lambda e: e.tensor_sub(
                out=AF32[:, FA_RW1:FA_RW1 + 1], in0=AF32[:, FA_M1:FA_M1 + 1],
                in1=AF32[:, FA_M2:FA_M2 + 1]))
            ctx.wait_ev("act", ev)
            ev = ctx.op("act", lambda e: e.activation(
                out=AF32[:, FA_RW1:FA_RW1 + 1], in_=AF32[:, FA_RW1:FA_RW1 + 1],
                func=AF.Sigmoid), inc=sACT)
            ctx.wait_ev("dve", ev)
            dsync(lambda e: e.tensor_scalar(
                out=AF32[:, FA_RW2:FA_RW2 + 1], in0=AF32[:, FA_RW1:FA_RW1 + 1],
                scalar1=-1.0, scalar2=1.0, op0=ALU.mult, op1=ALU.add))
            dsync(lambda e: e.tensor_scalar(
                out=AF32[:, FA_RT2:FA_RT2 + 8], in0=AF32[:, FA_RT1:FA_RT1 + 8],
                scalar1=AF32[:, FA_M2:FA_M2 + 1], scalar2=1.0, op0=ALU.is_ge,
                op1=ALU.mult))
            dsync(lambda e: e.tensor_scalar_mul(
                out=AF32[:, FA_RT0:FA_RT0 + 8], in0=AF32[:, FA_RT0:FA_RT0 + 8],
                scalar1=AF32[:, FA_RW1:FA_RW1 + 1]))
            dsync(lambda e: e.tensor_scalar_mul(
                out=AF32[:, FA_RT2:FA_RT2 + 8], in0=AF32[:, FA_RT2:FA_RT2 + 8],
                scalar1=AF32[:, FA_RW2:FA_RW2 + 1]))
            ev = dsync(lambda e: e.tensor_add(
                out=AF32[:, FA_RT0:FA_RT0 + 8], in0=AF32[:, FA_RT0:FA_RT0 + 8],
                in1=AF32[:, FA_RT2:FA_RT2 + 8]))
            if f"B_wrp_{seq}_{call}" in DEBUG_TAPS:
                ctx.op("dve", lambda e, t=t, tsub=tsub: e.tensor_copy(
                    out=A16c[:, 8192 + (t * 4 + tsub) * 8: 8192 + (t * 4 + tsub) * 8 + 8],
                    in_=AF32[:, FA_RT0:FA_RT0 + 8]), inc=sDVE)
            if f"B_lg_{seq}_{call}" in DEBUG_TAPS:
                ctx.op("dve", lambda e, t=t, tsub=tsub: e.tensor_copy(
                    out=A16c[:, 8320 + (t * 4 + tsub) * 8: 8320 + (t * 4 + tsub) * 8 + 8],
                    in_=lg), inc=sDVE)
            ctx.wait_ev("pe", ev)
            ctx.gw("pe", "bcb")
            ev = ctx.op("pe", lambda e: e.transpose(
                T["bcb"][0:8, 0:128], AF32[:, FA_RT0:FA_RT0 + 8],
                T["cst32"][:, 128:256]), inc=sPE)
            ctx.gs("rt0", ev)
            ctx.wait_ev("dve", ev)
            ev_wr = ctx.op("dve", lambda e, ts=ts, off=off: e.tensor_copy(
                out=AB16[0:8, BB_WRT + ts + off: BB_WRT + ts + off + 128],
                in_=T["bcb"][0:8, 0:128]), inc=sDVE)
            ctx.gs("bcb", ev_wr)
        ctx.gs("tn2t", (sPE, sPE.count))
    for t in range(NT):
        ts = t * T5
        ctx.wait("sp", sSt, sSt.count)
        ctx.gw("sp", "aot")
        ctx.op("sp", lambda e, ts=ts: e.dma_start(
            out=aot.rearrange("p (c t) -> p c t", c=DC),
            in_=T["aoS"].rearrange("(c p) t -> p c t", p=128)[:, :, ts:ts + T5]),
            inc=sW, n=16)
        ctx.gw("sp", "yt")
        ctx.op("sp", lambda e, ts=ts: e.dma_start(
            out=yt.rearrange("p (c t) -> p c t", c=DC),
            in_=T["yS"].rearrange("(c p) t -> p c t", p=128)[:, :, ts:ts + T5]),
            inc=sW, n=16)
        ctx.wait("pe", sW, sW.count)
        ev_s2 = None
        pend_s2 = []

        def emit_s2(oc, ev_sq, sq, skey):
            nonlocal ev_s2
            ctx.wait_ev("pe", ev_sq)
            if oc == 0:
                ctx.gw("pe", "rowsB")
            ev_s2 = ctx.op("pe", lambda e, oc=oc, sq=sq: e.matmul(
                T["rowB"][0:1, :], T["cst16"][:, 128:129], sq, start=(oc == 0),
                stop=(oc == DC - 1), skip_group_check=True), inc=sPE)
            ctx.gs(skey, ev_s2)

        for oc in range(DC):
            bk = oc % 2
            pairs = [(A16b[:, dc * 1024 + oc * 128: dc * 1024 + oc * 128 + 128],
                      aot[:, dc * 512: dc * 512 + 512]) for dc in range(DC)]
            ev = mm_chain(ctx, T, par, f"pp{bk}", T["pp"][bk][:, :], pairs)
            rsl = R[:, oc * S + ts: oc * S + ts + T5]
            hsl = htile[:, oc * 512: oc * 512 + 512]
            ysl = yt[:, oc * 512: oc * 512 + 512]
            ctx.wait_ev("dve", ev)
            ev_rupd = ctx.op("dve", lambda e, bk=bk, rsl=rsl: e.scalar_tensor_tensor(
                out=rsl, in0=T["pp"][bk][:, :], scalar=0.5, in1=rsl,
                op0=ALU.mult, op1=ALU.add), inc=sDVE)
            ev_h = ctx.op("dve", lambda e, bk=bk, hsl=hsl, ysl=ysl: e.tensor_add(
                out=hsl, in0=T["pp"][bk][:, :], in1=ysl), inc=sDVE)
            ctx.gs(f"pp{bk}", ev_h)
            skey = "sqB0" if oc % 2 == 0 else "sqB1"
            sq = A16c[:, 8192 + (oc % 2) * 512: 8192 + (oc % 2) * 512 + 512]
            ctx.wait_ev("act", ev_h)
            ctx.gw("act", skey)
            ev_sq = ctx.op("act", lambda e, hsl=hsl, sq=sq: e.activation(
                out=sq, in_=hsl, func=AF.Square), inc=sACT)
            pend_s2.append((oc, ev_sq, sq, skey))
            if len(pend_s2) == 2:
                emit_s2(*pend_s2.pop(0))
        emit_s2(*pend_s2.pop(0))
        ctx.gs("aot", (sPE, sPE.count))
        ctx.gs("yt", ev_h)
        if t >= 1:
            emit_routing(t - 1)
        ctx.wait_ev("dve", ev_s2)
        ev_rows = _rows_rstd(ctx, T, par, VC_EPSRMS, with_mean=False,
                             rkey="rowsB")
        ctx.wait_ev("pe", ev_rows)
        evH, bcH = _bcast_row(ctx, T, par, AF32[0:1, FA_ROW1:FA_ROW1 + 512],
                              evac_to=AF32[:, FA_BCB:FA_BCB + 512], evac_key="bcb2")
        ctx.gs("rows_sl", evH)
        ctx.wait_ev("dve", evH)
        ev_tn2 = None
        ctx.gw("dve", "sq1")
        ctx.gw("dve", "tn2t")
        for dc in range(DC):
            hsl = htile[:, dc * 512: dc * 512 + 512]
            tsl = tn2t[:, dc * 512: dc * 512 + 512]
            ctx.op("dve", lambda e, hsl=hsl: e.tensor_mul(
                out=AF32[:, FA_SQ1:FA_SQ1 + 512], in0=hsl, in1=bcH))
            ev_tn2 = ctx.op("dve", lambda e, tsl=tsl, dc=dc: e.tensor_scalar_mul(
                out=tsl, in0=AF32[:, FA_SQ1:FA_SQ1 + 512],
                scalar1=vecs[:, VC_MNW + dc:VC_MNW + dc + 1]), inc=sDVE)
        ctx.gs("sq1", ev_tn2)
        ctx.gs("bcb2", ev_tn2)
        ctx.wait_ev("gp", ev_tn2)
        ctx.op("gp", lambda e, ts=ts: e.dma_start(
            out=T["tnS"].rearrange("(c p) t -> p c t", p=128)[:, :, ts:ts + T5],
            in_=tn2t.rearrange("p (c t) -> p c t", c=DC)), inc=sSt, n=16)
    emit_routing(NT - 1)
    tap(ctx, T, f"B_R_{seq}_{call}", lambda: T["R"][:, :], [128, DC * S],
        np.float32, [ev_rupd])
    tap(ctx, T, f"B_tn2_{seq}_{call}", lambda: T["tnS"][:, :], [D, S],
        ml_dtypes.bfloat16, [])
    tap(ctx, T, f"B_wrt_{seq}_{call}",
        lambda: AB16[0:8, BB_WRT:BB_WRT + 2048], [8, 2048],
        ml_dtypes.bfloat16, [ev_wr])
    tap(ctx, T, f"B_wrp_{seq}_{call}", lambda: A16c[:, 8192:8192 + 128],
        [128, 128], ml_dtypes.bfloat16, [ev_wr])
    tap(ctx, T, f"B_lg_{seq}_{call}", lambda: A16c[:, 8320:8320 + 128],
        [128, 128], ml_dtypes.bfloat16, [ev_wr])
    return ev_rupd


def phaseC(ctx, T, seq, call, ev_r):
    par = call
    phase_join(ctx, T)
    R, AF32, AB16 = T["R"], T["AF32"], T["AB16"]
    A16a, A16b, A16c = T["A16a"], T["A16b"], T["A16c"]
    sPE, sACT, sDVE = T["sPE"][par], T["sACT"][par], T["sDVE"][par]
    sW, sSt = T["sW"][par], T["sSt"]
    b1b, b2b, cst16 = T["b1b"], T["b2b"], T["cst16"]
    tn2buf = A16a[:, 0:8192]
    w1buf = [A16a[:, 8192:12288], A16a[:, 12288:16384]]
    ev_rupd = None
    for t10 in range(2):
        t0 = t10 * 1024
        ctx.wait("sp", sSt, sSt.count)
        ctx.wait_ev("sp", ctx.g.get("prep_all"))
        ctx.gw("sp", "tn2buf")
        ctx.op("sp", lambda e, t0=t0: e.dma_start(
            out=tn2buf.rearrange("p (c t) -> p c t", c=DC),
            in_=T["tnS"].rearrange("(c p) t -> p c t", p=128)[:, :, t0:t0 + 1024]),
            inc=sW, n=16)
        ev_tn_ld = (sW, sW.count)
        for hcsl in range(8):
            h0 = hcsl * 512
            for eq in range(2):
                ev_w1s = [None, None]

                def w1_load(e4, eq=eq, h0=h0):
                    ee = eq * 4 + e4
                    ctx.gw("sp", f"w1b{e4 % 2}")
                    ctx.op("sp", lambda e, ee=ee, h0=h0, e4=e4: e.dma_start(
                        out=w1buf[e4 % 2].rearrange("p (c h) -> p c h", c=DC),
                        in_=T["wS1"][ee].rearrange("(c p) h -> p c h", p=128)
                        [:, :, h0:h0 + 512]), inc=sW, n=16)
                    ev_w1s[e4 % 2] = (sW, sW.count)
                w1_load(0)
                w1_load(1)
                ctx.gw("sp", "w2q")
                for e4b in range(4):
                    eeb = eq * 4 + e4b
                    ctx.op("sp", lambda e, eeb=eeb, h0=h0, e4b=e4b: e.dma_start(
                        out=A16b[:, e4b * 4096:(e4b + 1) * 4096]
                        .rearrange("p (c d) -> p c d", c=4),
                        in_=T["wS2"][eeb][h0:h0 + 512, :]
                        .rearrange("(c p) d -> p c d", p=128)), inc=sW, n=16)
                ev_w2 = (sW, sW.count)
                ev_hid = None
                for e4 in range(4):
                    ee = eq * 4 + e4
                    if e4 >= 2:
                        w1_load(e4)
                    ev_wb = None
                    selbanks = [T["bcb"], T["rowA"], T["rowB"]]
                    selkeys = ["bcb", "rows", "rowsB"]
                    for half in range(2):
                        sb = (e4 * 2 + half) % 3
                        ctx.gw("pe", selkeys[sb])
                        ev = ctx.op("pe", lambda e, ee=ee, t0=t0, half=half, sb=sb,
                                    selbanks=selbanks: e.matmul(
                            selbanks[sb][:, :],
                            cst16[0:8, 256 + ee * 128: 256 + ee * 128 + 128],
                            AB16[0:8, BB_WRT + t0 + half * 512:
                                 BB_WRT + t0 + half * 512 + 512],
                            start=True, stop=True, skip_group_check=True), inc=sPE)
                        ctx.wait_ev("act", ev)
                        wkey = "bca" if half == 0 else "bcb2"
                        dstw = AF32[:, (FA_BCA if half == 0 else FA_BCB):][:, 0:512]
                        ctx.gw("act", wkey)
                        ev_wb = ctx.op("act", lambda e, dstw=dstw, sb=sb,
                                       selbanks=selbanks: e.copy(
                            out=dstw, in_=selbanks[sb][:, :]), inc=sACT)
                        ctx.gs(selkeys[sb], ev_wb)
                    ctx.wait("pe", ev_w1s[e4 % 2][0], ev_w1s[e4 % 2][1])
                    ctx.wait("pe", ev_tn_ld[0], ev_tn_ld[1])
                    upbanks = [T["pp"][0], T["pp"][1], T["avb"]]
                    upkeys = ["pp0", "pp1", "avb"]
                    for hs in range(4):
                        for ts2 in range(2):
                            ub = (hs * 2 + ts2) % 3
                            pairs = [(w1buf[e4 % 2][:, dc * 512 + hs * 128:
                                                    dc * 512 + hs * 128 + 128],
                                      tn2buf[:, dc * 1024 + ts2 * 512:
                                             dc * 1024 + ts2 * 512 + 512])
                                     for dc in range(DC)]
                            ev = mm_chain(ctx, T, par, upkeys[ub], upbanks[ub][:, :],
                                          pairs)
                            gsl = (hs * 2 + ts2) % 3
                            gkey = ["sq0", "sq1", "sq2"][gsl]
                            gl = AF32[:, [FA_SQ0, FA_SQ1, FA_BCC][gsl]:][:, 0:512]
                            bc = b1b[:, ee * 32 + hcsl * 4 + hs:
                                     ee * 32 + hcsl * 4 + hs + 1]
                            ctx.wait_ev("act", ev)
                            ctx.gw("act", gkey)
                            ev_g = ctx.op("act", lambda e, ub=ub, gl=gl, bc=bc,
                                          upbanks=upbanks: e.activation(
                                              out=gl, in_=upbanks[ub][:, :],
                                              func=AF.Gelu, bias=bc), inc=sACT)
                            ctx.gs(upkeys[ub], ev_g)
                            ctx.wait_ev("dve", ev_g)
                            ctx.wait_ev("dve", ev_wb)
                            wbs = AF32[:, (FA_BCA if ts2 == 0 else FA_BCB):][:, 0:512]
                            hidsl = A16c[:, e4 * 4096 + hs * 1024 + ts2 * 512:
                                         e4 * 4096 + hs * 1024 + ts2 * 512 + 512]
                            ev_hid = ctx.op("dve", lambda e, gl=gl, wbs=wbs, hidsl=hidsl:
                                            e.tensor_mul(out=hidsl, in0=gl, in1=wbs),
                                            inc=sDVE)
                            ctx.gs(gkey, ev_hid)
                    ctx.gs("bca", ev_hid)
                    ctx.gs("bcb2", ev_hid)
                    ctx.gs(f"w1b{e4 % 2}", (sPE, sPE.count))
                ctx.wait("pe", ev_w2[0], ev_w2[1])
                ctx.wait("pe", sDVE, ev_hid[1])
                for dc in range(DC):
                    for ts2 in range(2):
                        bk = 2 + ((dc * 2 + ts2) % 2)
                        pairs = []
                        for e4 in range(4):
                            for hs in range(4):
                                pairs.append((
                                    A16b[:, e4 * 4096 + hs * 1024 + dc * 128:
                                         A16b.shape[-1] * 0 + e4 * 4096 + hs * 1024 + dc * 128 + 128],
                                    A16c[:, e4 * 4096 + hs * 1024 + ts2 * 512:
                                         e4 * 4096 + hs * 1024 + ts2 * 512 + 512]))
                        if hcsl == 7 and eq == 1:
                            pairs.append((
                                b2b[0:8, dc * 128:(dc + 1) * 128],
                                AB16[0:8, BB_WRT + t0 + ts2 * 512:
                                     BB_WRT + t0 + ts2 * 512 + 512]))
                        ev = mm_chain(ctx, T, par, f"pp{bk}", T["pp"][bk][:, :], pairs)
                        rsl = R[:, dc * S + t0 + ts2 * 512:
                                dc * S + t0 + ts2 * 512 + 512]
                        ctx.wait_ev("dve", ev)
                        ev_rupd = ctx.op("dve", lambda e, bk=bk, rsl=rsl:
                                         e.scalar_tensor_tensor(
                                             out=rsl, in0=T["pp"][bk][:, :], scalar=0.5,
                                             in1=rsl, op0=ALU.mult, op1=ALU.add),
                                         inc=sDVE)
                        ctx.gs(f"pp{bk}", ev_rupd)
                ctx.gs("w2q", (sPE, sPE.count))
                for e4 in range(4):
                    pass
        ctx.gs("tn2buf", (sPE, sPE.count))
    tap(ctx, T, f"C_R_{seq}_{call}", lambda: T["R"][:, :], [128, DC * S],
        np.float32, [ev_rupd])
    return ev_rupd


def phase_mid(ctx, T, seq, ev_r):
    par = 0
    phase_join(ctx, T)
    R, AF32 = T["R"], T["AF32"]
    A16a, A16b, A16c = T["A16a"], T["A16b"], T["A16c"]
    aqkv, awo, vecs = T["aqkv"], T["awo"], T["vecs"]
    sPE, sACT, sDVE = T["sPE"][par], T["sACT"][par], T["sDVE"][par]
    for eng in ("pe", "act", "dve"):
        ctx.wait_ev(eng, ctx.g.get("prep_attn"))
    y2 = A16a[:, 0:4096]
    q2 = A16b[:, 0:2048]
    k2 = A16b[:, 2048:4096]
    z2 = A16b[:, 4096:6144]
    v2 = A16c[:, 0:1024]
    for t in range(NT):
        ts = t * T5

        def rtile(dc, ts=ts):
            return R[:, dc * S + ts: dc * S + ts + T5]
        ev_rows = _ln_stats(ctx, T, par, rtile, True, VC_EPSLN,
                            sqa=A16c[:, 1024:2048], sqkeys=("sqM0", "sqM1"))
        ctx.wait_ev("pe", ev_rows)
        evA, bcA = _bcast_row(ctx, T, par, AF32[0:1, FA_ROW1:FA_ROW1 + 512],
                              evac_to=AF32[:, FA_BCA:FA_BCA + 512], evac_key="bca")
        evB, bcB = _bcast_row(ctx, T, par, AF32[0:1, FA_ROW0:FA_ROW0 + 512])
        ctx.gs("rows_sl", evB)
        ctx.wait_ev("dve", evA)
        ctx.wait_ev("dve", evB)
        ctx.gw("dve", "y2")
        ctx.gw("dve", "sq1")
        ev_y2 = None
        for dc in range(DC):
            rsl = rtile(dc)
            ysl = y2[:, dc * 512: dc * 512 + 512]
            ctx.op("dve", lambda e, rsl=rsl: e.tensor_mul(
                out=AF32[:, FA_SQ1:FA_SQ1 + 512], in0=rsl, in1=bcA))
            ev_y2 = ctx.op("dve", lambda e, ysl=ysl: e.tensor_add(
                out=ysl, in0=AF32[:, FA_SQ1:FA_SQ1 + 512], in1=bcB), inc=sDVE)
        ctx.gs("bcb", ev_y2)
        ctx.gs("bca", ev_y2)
        ctx.gs("sq1", ev_y2)
        ctx.wait_ev("pe", ev_y2)
        for which, bias_c in ((0, VC_BQ), (1, VC_BK)):
            bk = which
            pairs = [(aqkv[:, which * 512 + dc * 64: which * 512 + dc * 64 + 64],
                      y2[:, dc * 512: dc * 512 + 512]) for dc in range(DC)]
            ev = mm_chain(ctx, T, par, f"pp{bk}", T["pp"][bk][0:64, :], pairs)
            dst = (q2 if which == 0 else k2)[0:64, ts:ts + T5]
            ctx.wait_ev("act", ev)
            ev_qk = ctx.op("act", lambda e, bk=bk, dst=dst, bias_c=bias_c: e.activation(
                out=dst, in_=T["pp"][bk][0:64, :], func=AF.Identity,
                bias=vecs[0:64, bias_c:bias_c + 1]), inc=sACT)
            ctx.gs(f"pp{bk}", ev_qk)
        for tsub in range(4):
            tc = t * 4 + tsub
            pairs = [(y2[:, dc * 512 + tsub * 128: dc * 512 + tsub * 128 + 128],
                      aqkv[:, 2 * 512 + dc * 64: 2 * 512 + dc * 64 + 64])
                     for dc in range(DC)]
            ev = mm_chain(ctx, T, par, "pp2", T["pp"][2][:, 0:64], pairs)
            ctx.wait_ev("act", ev)
            ev_v2 = ctx.op("act", lambda e, tc=tc: e.copy(
                out=v2[:, tc * 64: tc * 64 + 64], in_=T["pp"][2][:, 0:64]), inc=sACT)
            ctx.gs("pp2", ev_v2)
        ctx.gs("y2", (sPE, sPE.count))
    ctx.wait("pe", sACT, sACT.count)
    ev_ao = _softmax_pair(
        ctx, T, par, [0],
        kT_fn=lambda h, kc: k2[0:64, kc * 128:kc * 128 + 128],
        qT_fn=lambda h, qs: q2[0:64, qs * T5:qs * T5 + T5],
        v_fn=lambda h, kc: v2[:, kc * 64: kc * 64 + 64],
        ao_fn=lambda h, qs: z2[0:64, qs * T5:qs * T5 + T5],
        qfull=q2[0:64, :], kfull=k2[0:64, :], sqarea=A16c[:, 2048:6144],
        bias_col=VC_BV)
    ctx.wait_ev("pe", ev_ao)
    ev_rupd = None
    for t in range(NT):
        ts = t * T5
        for oc in range(DC):
            bk = oc % 2
            ev = mm_chain(ctx, T, par, f"pp{bk}", T["pp"][bk][:, :],
                          [(awo[:, oc * 128:(oc + 1) * 128], z2[0:64, ts:ts + T5])])
            rsl = R[:, oc * S + ts: oc * S + ts + T5]
            ctx.wait_ev("dve", ev)
            ev_rupd = ctx.op("dve", lambda e, bk=bk, rsl=rsl, oc=oc:
                             e.scalar_tensor_tensor(
                                 out=rsl, in0=T["pp"][bk][:, :],
                                 scalar=vecs[:, VC_BO + oc:VC_BO + oc + 1],
                                 in1=rsl, op0=ALU.add, op1=ALU.add), inc=sDVE)
            ctx.gs(f"pp{bk}", ev_rupd)
    tap(ctx, T, f"M_R_{seq}", lambda: T["R"][:, :], [128, DC * S], np.float32,
        [ev_rupd])
    return ev_rupd



def _host_inputs(inp):
    f32 = np.float32
    bf = ml_dtypes.bfloat16
    x = np.asarray(inp["L0_attn_x"])
    xT = np.ascontiguousarray(x.transpose(0, 2, 1))

    def tr(a):
        return np.ascontiguousarray(np.asarray(a, f32).T)
    com = {
        "a_wq": np.ascontiguousarray(np.asarray(inp["L0_attn_wq"], f32)),
        "a_wk": np.ascontiguousarray(np.asarray(inp["L0_attn_wk"], f32)),
        "a_wv": np.ascontiguousarray(np.asarray(inp["L0_attn_wv"], f32)),
        "a_wo": np.ascontiguousarray(np.asarray(inp["L0_attn_wo"], f32)),
        "f_wqT": tr(inp["L0_ffn_wq"]),
        "f_wkT": tr(inp["L0_ffn_wk"]),
        "f_wvT": tr(inp["L0_ffn_wv"]),
        "f_woT": tr(inp["L0_ffn_wo"]),
        "gateT": tr(inp["L0_ffn_gate_w"]),
        "w1T": np.ascontiguousarray(np.asarray(inp["L0_ffn_w1"], f32).transpose(0, 2, 1)),
        "w2T": np.ascontiguousarray(np.asarray(inp["L0_ffn_w2"], f32).transpose(0, 2, 1)),
        "b2": np.ascontiguousarray(np.asarray(inp["L0_ffn_b2"], f32)),
    }
    b1 = np.asarray(inp["L0_ffn_b1"], f32)
    com["b1f"] = np.ascontiguousarray(
        b1.reshape(NEXP, 32, 128).transpose(2, 0, 1).reshape(128, NEXP * 32))
    vecs = np.zeros((128, 64), f32)
    vecs[:, VC_ANW:VC_ANW + 8] = np.asarray(inp["L0_ffn_attn_norm_w"], f32).reshape(8, 128).T
    vecs[:, VC_MNW:VC_MNW + 8] = np.asarray(inp["L0_ffn_moe_norm_w"], f32).reshape(8, 128).T
    vecs[:, VC_BO:VC_BO + 8] = np.asarray(inp["L0_attn_bo"], f32).reshape(8, 128).T
    vecs[0:64, VC_BQ] = np.asarray(inp["L0_attn_bq"], f32)
    vecs[0:64, VC_BK] = np.asarray(inp["L0_attn_bk"], f32)
    vecs[0:64, VC_BV] = np.asarray(inp["L0_attn_bv"], f32)
    vecs[:, VC_EPSLN] = 1e-5
    vecs[:, VC_EPSRMS] = 1e-6
    com["vecs"] = vecs
    cst32 = np.zeros((128, 256), f32)
    cst32[:, 0:128] = 1.0
    cst32[:, 128:256] = np.eye(128, dtype=f32)
    com["cst32"] = cst32
    cst16 = np.zeros((128, 1280), f32)
    Rm = np.zeros((128, 128), f32)
    for hh in range(2):
        bb = hh * 64
        Rm[bb:bb + 32, bb + 32:bb + 64] = -np.eye(32)
        Rm[bb + 32:bb + 64, bb:bb + 32] = np.eye(32)
    cst16[:, 0:128] = Rm.T
    cst16[:, 128:256] = 1.0
    for ee in range(NEXP):
        cst16[ee, 256 + ee * 128: 256 + (ee + 1) * 128] = 1.0
    com["cst16"] = cst16.astype(bf)
    kk = np.arange(128)[:, None]
    cc = np.arange(1280)[None, :]
    com["maskbig"] = (cc >= kk + 384).astype(bf)
    inv_freq = 1.0 / (10000.0 ** (np.arange(0, HD, 2, dtype=f32) / HD))
    pos = np.arange(S, dtype=f32)
    freqs = pos[:, None] * inv_freq[None, :]
    emb = np.concatenate([freqs, freqs], axis=1)
    cos_t = np.cos(emb).T.astype(f32)
    sin_t = np.sin(emb).T.astype(f32)
    cossin = np.zeros((128, 4096), f32)
    cossin[0:64, 0:2048] = cos_t
    cossin[64:128, 0:2048] = cos_t
    cossin[0:64, 2048:4096] = sin_t
    cossin[64:128, 2048:4096] = sin_t
    com["cossin"] = cossin.astype(bf)
    return xT, com


def _launch(inp):
    xT, com = _host_inputs(inp)
    nc, ctx = build_nc()
    in_maps = []
    for c in range(8):
        m = dict(com)
        m["xT"] = np.ascontiguousarray(xT[c * NSEQ:(c + 1) * NSEQ])
        in_maps.append(m)
    return run_bass_kernel_spmd(nc, in_maps, core_ids=list(range(8)))


def kernel(**inp):
    res = _launch(inp)
    outs = [res.results[c]["out"] for c in range(8)]
    full = np.concatenate(outs, axis=0)
    return np.ascontiguousarray(full.transpose(0, 2, 1))
